# version 09
# baseline (speedup 1.0000x reference)
import numpy as np
import ml_dtypes
import concourse.bass as bass
import concourse.mybir as mybir
from concourse.bass_utils import run_bass_kernel_spmd
from contextlib import ExitStack

F32 = mybir.dt.float32
BF16 = mybir.dt.bfloat16
I32 = mybir.dt.int32
AF = mybir.ActivationFunctionType
ALU = mybir.AluOpType
AX = mybir.AxisListType

S = 2048
DM = 1024
NT = S // 128
NC_ = DM // 128
BF = ml_dtypes.bfloat16
L0A_D_STRIDE0 = False


def _lhsT_tiles(wT, n_out_chunks=None):
    d_in, d_out = wT.shape
    ni, no = d_in // 128, d_out // 128
    a = wT.reshape(ni, 128, no, 128)
    return np.ascontiguousarray(a.transpose(2, 1, 0, 3))


def _rhs_tiles(wT):
    d_in, d_out = wT.shape
    return np.ascontiguousarray(wT.reshape(d_in // 128, 128, d_out))


def _chunk_cols(v):
    return np.ascontiguousarray(v.reshape(-1, 128).T)


def _rep(v, rows=128):
    return np.ascontiguousarray(np.broadcast_to(v[None, :], (rows, v.shape[0])))


def _host_consts():
    c = {}
    c["c_id"] = np.eye(128, dtype=BF)
    misc = np.zeros((128, 4), np.float32)
    misc[:, 0] = 1e-5
    c["c_misc"] = misc
    def l0mask(g0):
        m = np.zeros((128, 512), np.float32)
        for band in (0, 64):
            for tl in range(47):
                for o1 in range(16):
                    for sl in range(32):
                        o2 = sl + 15 - tl
                        ok = 0 <= o2 <= 15
                        if g0:
                            ok = ok and (tl >= 15) and (o1 <= sl)
                        if ok:
                            m[band + tl, o1 * 32 + sl] = 1.0
        return m.astype(BF)
    c["c_l0m_gen"] = l0mask(False)
    c["c_l0m_g0"] = l0mask(True)
    tri = np.zeros((128, 47), np.float32)
    for band in (0, 64):
        for k in range(47):
            tri[band + k, k:47] = 1.0
    c["c_l0tri"] = tri.astype(BF)
    def l0sel(g0):
        s = np.zeros((128, 512), np.float32)
        for band in (0, 64):
            for o1 in range(16):
                for sl in range(32):
                    col = o1 * 32 + sl
                    lo, hi = sl, sl + 15
                    if g0:
                        lo = max(lo, 15)
                        if o1 > sl:
                            lo, hi = 1, 0
                    s[band + 0, col] += -1.0
                    if lo <= hi:
                        s[band + lo, col] += 1.0
                        if hi + 1 <= 46:
                            s[band + hi + 1, col] += -1.0
        return (s * 1e30).astype(BF)
    c["c_l0sel_gen"] = l0sel(False)
    c["c_l0sel_g0"] = l0sel(True)
    pm = np.zeros((128, 512), np.float32)
    for band in (0, 64):
        for o1 in range(16):
            for sl in range(32):
                r = 15 + sl - o1
                if 0 <= r <= 46:
                    pm[band + r, o1 * 32 + sl] = 1.0
    c["c_l0pmat"] = pm.astype(BF)
    oab = np.zeros((128, 2), np.float32)
    oab[0:47, 0] = 1.0
    oab[64:111, 1] = 1.0
    c["c_l0ones"] = oab.astype(BF)
    rp = np.zeros((2, 128), np.float32)
    rp[0, 0:64] = 1.0
    rp[1, 64:128] = 1.0
    c["c_l0rep"] = rp
    on = np.zeros((128, 64), np.float32)
    on[64, :] = 1.0
    c["c_ones64"] = on
    m1 = np.zeros((128, 144), np.float32)
    for r in range(128):
        for cq in range(144):
            if 0 <= cq - r <= 16:
                m1[r, cq] = -1e30
    c["c_l1mask"] = m1.astype(BF)
    return c


def _prep_core_inputs(core, inp, consts):
    d = dict(consts)
    d["x"] = np.ascontiguousarray(inp["L0_attn_x"][core])
    ws = [np.ascontiguousarray(inp["L0_attn_W" + k].T) for k in ("q", "k", "kp", "v")]
    a = np.stack([w.reshape(8, 128, 8, 128) for w in ws], axis=0)
    d["l0a_wp"] = np.ascontiguousarray(a.transpose(3, 2, 0, 1, 4))
    d["l0a_wo"] = _rhs_tiles(np.ascontiguousarray(inp["L0_attn_Wo"].T))
    d["l0a_gamma"] = _rep(inp["L0_attn_gamma"])
    d["l0a_beta"] = _rep(inp["L0_attn_beta"])
    d["l0f_wg"] = _lhsT_tiles(np.ascontiguousarray(inp["L0_ffn_w_gate"].T))
    d["l0f_wu"] = _lhsT_tiles(np.ascontiguousarray(inp["L0_ffn_w_up"].T))
    d["l0f_wd"] = _rhs_tiles(np.ascontiguousarray(inp["L0_ffn_w_down"].T))
    wip = np.ascontiguousarray(inp["L1_attn_in_proj_weight"].T)
    d["l1a_wqk"] = _lhsT_tiles(np.ascontiguousarray(wip[:, 0:2048]))
    d["l1a_wv"] = _rhs_tiles(np.ascontiguousarray(wip[:, 2048:3072]))
    b = inp["L1_attn_in_proj_bias"]
    d["l1a_bqk"] = _chunk_cols(b[0:2048])
    bv = np.zeros((1040,), np.float32)
    for h in range(16):
        bv[h * 65:h * 65 + 64] = b[2048 + h * 64: 2048 + (h + 1) * 64]
    d["l1a_bv"] = _rep(bv)
    d["l1a_wo"] = _rhs_tiles(np.ascontiguousarray(inp["L1_attn_out_proj_weight"].T))
    d["l1a_bo"] = _rep(inp["L1_attn_out_proj_bias"])
    d["l1f_w1"] = _lhsT_tiles(np.ascontiguousarray(inp["L1_ffn_w1_w"].T))
    d["l1f_w2"] = _lhsT_tiles(np.ascontiguousarray(inp["L1_ffn_w2_w"].T))
    d["l1f_h1"] = _lhsT_tiles(np.ascontiguousarray(inp["L1_ffn_h1_w"].T))
    d["l1f_h2"] = _lhsT_tiles(np.ascontiguousarray(inp["L1_ffn_h2_w"].T))
    d["l1f_w3"] = _rhs_tiles(np.ascontiguousarray(inp["L1_ffn_w3_w"].T))
    d["l1f_s1"] = _chunk_cols(inp["L1_ffn_w1_s"])
    d["l1f_s2"] = _chunk_cols(inp["L1_ffn_w2_s"])
    d["l1f_sh1"] = _chunk_cols(inp["L1_ffn_h1_s"])
    d["l1f_sh2"] = _chunk_cols(inp["L1_ffn_h2_s"])
    d["l1f_s3"] = _rep(inp["L1_ffn_w3_s"])
    return d



class Ch:

    def __init__(self, nc, stack, name):
        self.sem = stack.enter_context(nc.semaphore(name))
        self.n = 0

    def inc(self, instr, k=1):
        instr.then_inc(self.sem, k)
        self.n += k
        return instr

    def mark(self):
        return self.n


class Env:
    pass


def build_nc(blocks=("l0a", "l0f", "l1a", "l1f"), dev_taps=()):
    nc = bass.Bass("TRN2", target_bir_lowering=False)
    e = Env()
    e.nc = nc
    e.blocks = blocks
    e.want_taps = set(dev_taps)

    dt_in = {
        "x": F32, "c_id": BF16, "c_misc": F32,
        "c_l0m_gen": BF16, "c_l0m_g0": BF16, "c_l0pmat": BF16, "c_l0ones": BF16,
        "c_l0tri": BF16, "c_l0sel_gen": BF16, "c_l0sel_g0": BF16,
        "c_l0rep": F32, "c_ones64": F32, "c_l1mask": BF16,
        "l0a_wp": F32, "l0a_wo": F32, "l0a_gamma": F32, "l0a_beta": F32,
        "l0f_wg": I32, "l0f_wu": I32, "l0f_wd": I32,
        "l1a_wqk": F32, "l1a_wv": F32, "l1a_bqk": F32, "l1a_bv": F32,
        "l1a_wo": F32, "l1a_bo": F32,
        "l1f_w1": F32, "l1f_w2": F32, "l1f_h1": F32, "l1f_h2": F32, "l1f_w3": F32,
        "l1f_s1": F32, "l1f_s2": F32, "l1f_sh1": F32, "l1f_sh2": F32, "l1f_s3": F32,
    }
    shapes = {
        "x": [S, DM], "c_id": [128, 128], "c_misc": [128, 4],
        "c_l0m_gen": [128, 512], "c_l0m_g0": [128, 512], "c_l0pmat": [128, 512],
        "c_l0tri": [128, 47], "c_l0sel_gen": [128, 512], "c_l0sel_g0": [128, 512],
        "c_l0ones": [128, 2], "c_l0rep": [2, 128], "c_ones64": [128, 64],
        "c_l1mask": [128, 144],
        "l0a_wp": [8, 128, 4, 8, 128], "l0a_wo": [8, 128, 1024],
        "l0a_gamma": [128, 1024], "l0a_beta": [128, 1024],
        "l0f_wg": [24, 128, 8, 128], "l0f_wu": [24, 128, 8, 128], "l0f_wd": [24, 128, 1024],
        "l1a_wqk": [16, 128, 8, 128], "l1a_wv": [8, 128, 1024],
        "l1a_bqk": [128, 16], "l1a_bv": [128, 1040],
        "l1a_wo": [8, 128, 1024], "l1a_bo": [128, 1024],
        "l1f_w1": [32, 128, 8, 128], "l1f_w2": [32, 128, 8, 128],
        "l1f_h1": [32, 128, 32, 128], "l1f_h2": [32, 128, 32, 128],
        "l1f_w3": [32, 128, 1024],
        "l1f_s1": [128, 32], "l1f_s2": [128, 32], "l1f_sh1": [128, 32],
        "l1f_sh2": [128, 32], "l1f_s3": [128, 1024],
    }
    e.hbm = {}
    needed = {"x", "c_id", "c_misc"}
    if "l0a" in blocks:
        needed |= {"l0a_wp", "l0a_wo", "l0a_gamma", "l0a_beta",
                   "c_l0tri", "c_l0sel_gen", "c_l0sel_g0", "c_l0ones", "c_l0rep"}
    if "l0f" in blocks:
        needed |= {"l0f_wg", "l0f_wu", "l0f_wd"}
    if "l1a" in blocks:
        needed |= {"l1a_wqk", "l1a_wv", "l1a_bqk", "l1a_bv", "l1a_wo", "l1a_bo",
                   "c_ones64", "c_l1mask"}
    if "l1f" in blocks:
        needed |= {"l1f_w1", "l1f_w2", "l1f_h1", "l1f_h2", "l1f_w3",
                   "l1f_s1", "l1f_s2", "l1f_sh1", "l1f_sh2", "l1f_s3"}
    e.needed = needed
    for k in sorted(needed):
        e.hbm[k] = nc.dram_tensor(k, shapes[k], dt_in[k], kind="ExternalInput").ap()
    e.out = nc.dram_tensor("out", [S, DM], F32, kind="ExternalOutput").ap()
    if "l1f" in blocks:
        e.spill = nc.dram_tensor("resid_spill", [S, DM], F32, kind="Internal").ap()

    with ExitStack() as stack:
        e.stack = stack
        sb = lambda name, shape, dt: stack.enter_context(nc.sbuf_tensor(name, shape, dt))
        e.resid_cm = nc.sbuf_tensor("resid", [128, NT * 1024], F32, side="right")
        e.resid = e.resid_cm.__enter__()
        e.c_id = sb("cid", [128, 128], BF16)
        e.c_misc = sb("cmisc", [128, 4], F32)
        e.mu = sb("mu", [128, NT], F32)
        e.rstd = sb("rstd", [128, NT], F32)
        e.lnb = sb("lnb", [128, NT], F32)
        e.bn = sb("bn", [128, NT * 12], F32)
        e.t = Ch(nc, stack, "t")
        e.s = Ch(nc, stack, "s")
        e.v = Ch(nc, stack, "v")
        e.y = Ch(nc, stack, "y")
        e.g = Ch(nc, stack, "g")
        e.bufs = {}
        e.gchan = {}

        with nc.body("main"):
            e.blk = _FlatBlk(nc)
            _emit_program(e)

    return e


class _FlatBlk:
    def __init__(self, nc):
        self.nc = nc

    def tensor(self, fn):
        fn(self.nc.tensor)

    def scalar(self, fn):
        fn(self.nc.scalar)

    def vector(self, fn):
        fn(self.nc.vector)

    def sync(self, fn):
        fn(self.nc.sync)

    def gpsimd(self, fn):
        fn(self.nc.gpsimd)


def _gch(e, name):
    if name not in e.gchan:
        e.gchan[name] = Ch(e.nc, e.stack, "gd_" + name)
    return e.gchan[name]


def _vfm_shift_view(vfm, g):
    import bass_rust as _br
    a = vfm[:, 0:512].copy()
    a.ap = _br.VecI64Pair([[2063, 128], [-1, 16], [1, 32]])
    a.offset = 15 + 32 * g
    return a


def _recip_fastnr(v, out, in_, scratch):
    import concourse.dve_ops as _dve
    v.reciprocal_approx_fast(out=scratch, in_=in_)
    v.drain()
    return v._custom_dve(_dve.RECIPROCAL_APPROX_NR, out=out, in0=in_, in1=scratch, s0=2.0)


def _need(eng, *marks):
    cache = getattr(eng, "_wait_cache", None)
    if cache is None:
        cache = eng._wait_cache = {}
    for ch, m in marks:
        if m > 0 and cache.get(id(ch), -1) < m:
            eng.wait_ge(ch.sem, m)
            cache[id(ch)] = m


def _tap(e, name, ap, wait_marks):
    if name not in e.want_taps:
        return
    nc, blk = e.nc, e.blk
    shape = [ap.shape[0], int(np.prod(ap.shape[1:]))]
    t = nc.dram_tensor("tap_" + name, shape, ap.tensor.dtype, kind="ExternalOutput").ap()
    def y_t(y):
        _need(y, *wait_marks)
        e.y.inc(y.dma_start(out=t, in_=ap), 16)
    blk.sync(y_t)


def _emit_program(e):
    nc, blk = e.nc, e.blk

    def y_start(y):
        for t in range(NT):
            ins = y.dma_start(out=e.resid[:, t * 1024:(t + 1) * 1024],
                              in_=e.hbm["x"][t * 128:(t + 1) * 128, :])
            e.y.inc(ins, 16)
        e.y.inc(y.dma_start(out=e.c_id[:, :], in_=e.hbm["c_id"]), 16)
        e.y.inc(y.dma_start(out=e.c_misc[:, :], in_=e.hbm["c_misc"]), 16)
    blk.sync(y_start)
    e.m_x_loaded = (e.y, e.y.mark())

    first = True
    for b in e.blocks:
        if b == "l0a":
            _emit_l0a(e, first)
        elif b == "l0f":
            _emit_l0f(e, first)
        elif b == "l1a":
            _emit_l1a(e, first)
        elif b == "l1f":
            _emit_l1f(e, first)
        first = False

    if "l1f" not in e.blocks:
        mt, ms, mv = (e.t, e.t.mark()), (e.s, e.s.mark()), (e.v, e.v.mark())
        def y_out(y):
            _need(y, mt, ms, mv)
            for t in range(NT):
                ins = y.dma_start(out=e.out[t * 128:(t + 1) * 128, :],
                                  in_=e.resid[:, t * 1024:(t + 1) * 1024])
                e.y.inc(ins, 16)
            y.wait_ge(e.y.sem, e.y.mark())
        blk.sync(y_out)



def _emit_ln_stats(e, after_marks):
    nc, blk = e.nc, e.blk

    def v_stats(v):
        _need(v, after_marks.get("resid", (e.v, 0)), e.m_x_loaded)
        v.drain()
        for t in range(NT):
            v.bn_stats(e.bn[:, t * 12:t * 12 + 6], e.resid[:, t * 1024:t * 1024 + 512])
            v.bn_stats(e.bn[:, t * 12 + 6:t * 12 + 12], e.resid[:, t * 1024 + 512:t * 1024 + 1024])
        bnv = e.bn[:, :].rearrange("p (t g k) -> p t g k", g=4, k=3)
        msum, m2sum = e.mu[:, :], e.lnb[:, :]
        sqsum = e.rstd[:, :]
        v.drain()
        v.tensor_reduce(out=msum, in_=bnv[:, :, :, 1], axis=AX.X, op=ALU.add)
        v.tensor_reduce(out=m2sum, in_=bnv[:, :, :, 2], axis=AX.X, op=ALU.add)
        v.tensor_tensor(out=bnv[:, :, :, 0], in0=bnv[:, :, :, 1], in1=bnv[:, :, :, 1],
                        op=ALU.mult)
        v.drain()
        v.tensor_reduce(out=sqsum, in_=bnv[:, :, :, 0], axis=AX.X, op=ALU.add)
        v.tensor_scalar_mul(out=e.mu[:, :], in0=msum, scalar1=0.25)
        v.drain()
        v.tensor_scalar_mul(out=sqsum, in0=sqsum, scalar1=0.25)
        v.drain()
        v.scalar_tensor_tensor(out=sqsum, in0=m2sum, scalar=1.0 / 1024, in1=sqsum,
                               op0=ALU.mult, op1=ALU.add)
        v.drain()
        v.tensor_tensor(out=m2sum, in0=e.mu[:, :], in1=e.mu[:, :], op=ALU.mult)
        v.drain()
        ins = v.tensor_tensor(out=e.rstd[:, :], in0=sqsum, in1=m2sum, op=ALU.subtract)
        e.v.inc(ins)
    blk.vector(v_stats)
    mv = (e.v, e.v.mark())

    def s_rstd(s):
        _need(s, mv)
        s.activation(out=e.rstd[:, :], in_=e.rstd[:, :], func=AF.Ln,
                     bias=e.c_misc[:, 0:1], scale=1.0)
        s.drain()
        s.activation(out=e.rstd[:, :], in_=e.rstd[:, :], func=AF.Exp, scale=-0.5)
        ins = s.drain()
        e.s.inc(ins)
    blk.scalar(s_rstd)
    ms = (e.s, e.s.mark())

    def v_lnb(v):
        _need(v, ms)
        v.scalar_tensor_tensor(out=e.lnb[:, :], in0=e.mu[:, :], scalar=-1.0,
                               in1=e.rstd[:, :], op0=ALU.mult, op1=ALU.mult)
        e.v.inc(v.drain())
    blk.vector(v_lnb)
    return (e.v, e.v.mark())


def _emit_ln_apply(e, ph, xn_fm, pad, t0, nt, tag, stats_mark):
    nc, blk = e.nc, e.blk
    padf, padb = (pad, pad) if isinstance(pad, int) else pad
    W = padf + padb + nt * 128
    xn_tm = ph.enter_context(nc.sbuf_tensor("xn_tm_" + tag, [128, 1024], BF16))
    ps_tr = ph.enter_context(nc.psum_tensor("ps_tr_" + tag, [128, 512], F32))
    ps_tr2 = ph.enter_context(nc.psum_tensor("ps_tr2_" + tag, [128, 512], F32))

    if padf or padb:
        def v_pad(v):
            xv = xn_fm[:, :].rearrange("p (c w) -> p c w", w=W)
            if padf:
                v.memset(xv[:, :, 0:padf], 0)
            ins = v.memset(xv[:, :, padf + nt * 128:W], 0)
            e.v.inc(ins)
        blk.vector(v_pad)

    marks = {}
    for i in range(nt):
        t = t0 + i
        def s_apply(s, t=t, i=i):
            _need(s, stats_mark)
            if i >= 1:
                _need(s, marks[("tr", i - 1)])
            ins = s.activation(out=xn_tm[:, :], in_=e.resid[:, t * 1024:(t + 1) * 1024],
                               func=AF.Identity, bias=e.lnb[:, t:t + 1],
                               scale=e.rstd[:, t:t + 1])
            e.s.inc(ins)
        blk.scalar(s_apply)
        marks[("ap", i)] = (e.s, e.s.mark())

        def t_tr(tn, i=i):
            _need(tn, marks[("ap", i)])
            if i >= 1:
                _need(tn, marks[("ev", i - 1)])
            for c in range(NC_):
                ps = ps_tr if c < 4 else ps_tr2
                ins = tn.matmul(ps[:, (c % 4) * 128:(c % 4) * 128 + 128],
                                xn_tm[:, c * 128:(c + 1) * 128], e.c_id[:, :],
                                start=True, stop=True)
            e.t.inc(ins)
        blk.tensor(t_tr)
        marks[("tr", i)] = (e.t, e.t.mark())

        def s_ev(s, i=i):
            _need(s, marks[("tr", i)])
            xv = xn_fm[:, :].rearrange("p (c w) -> p c w", w=W)
            s.copy(out=xv[:, 0:4, padf + i * 128:padf + (i + 1) * 128],
                   in_=ps_tr[:, :].rearrange("p (c w) -> p c w", w=128))
            ins = s.copy(out=xv[:, 4:8, padf + i * 128:padf + (i + 1) * 128],
                         in_=ps_tr2[:, :].rearrange("p (c w) -> p c w", w=128))
            e.s.inc(ins)
        blk.scalar(s_ev)
        marks[("ev", i)] = (e.s, e.s.mark())

    return {"fm": (e.s, e.s.mark()), "stats": stats_mark}


def _emit_ln_apply_from_spill(e, ph, xn_fm, t0, nt, tag, stats_mark, m_spill):
    nc, blk = e.nc, e.blk
    W = nt * 128
    xn_tm = ph.enter_context(nc.sbuf_tensor("xn_tm_" + tag, [128, 1024], BF16))
    rstg = [ph.enter_context(nc.sbuf_tensor(f"rstg_{tag}{i}", [128, 1024], F32)) for i in range(2)]
    ps_tr = ph.enter_context(nc.psum_tensor("ps_tr_" + tag, [128, 512], F32))
    ps_tr2 = ph.enter_context(nc.psum_tensor("ps_tr2_" + tag, [128, 512], F32))
    chs = [_gch(e, f"lnspill_{tag}{i}") for i in range(2)]

    marks = {}
    for i in range(nt):
        t = t0 + i
        def y_ld(y, t=t, i=i):
            _need(y, m_spill)
            if i >= 2:
                _need(y, marks[("ap", i - 2)])
            ins = y.dma_start(out=rstg[i % 2][:, :], in_=e.spill[t * 128:(t + 1) * 128, :])
            chs[i % 2].inc(ins, 16)
        blk.sync(y_ld)
        marks[("ld", i)] = (chs[i % 2], chs[i % 2].mark())

        def s_apply(s, t=t, i=i):
            _need(s, stats_mark, marks[("ld", i)])
            if i >= 1:
                _need(s, marks[("tr", i - 1)])
            ins = s.activation(out=xn_tm[:, :], in_=rstg[i % 2][:, :],
                               func=AF.Identity, bias=e.lnb[:, t:t + 1],
                               scale=e.rstd[:, t:t + 1])
            e.s.inc(ins)
        blk.scalar(s_apply)
        marks[("ap", i)] = (e.s, e.s.mark())

        def t_tr(tn, i=i):
            _need(tn, marks[("ap", i)])
            if i >= 1:
                _need(tn, marks[("ev", i - 1)])
            for c in range(NC_):
                ps = ps_tr if c < 4 else ps_tr2
                ins = tn.matmul(ps[:, (c % 4) * 128:(c % 4) * 128 + 128],
                                xn_tm[:, c * 128:(c + 1) * 128], e.c_id[:, :],
                                start=True, stop=True)
            e.t.inc(ins)
        blk.tensor(t_tr)
        marks[("tr", i)] = (e.t, e.t.mark())

        def s_ev(s, i=i):
            _need(s, marks[("tr", i)])
            xv = xn_fm[:, :].rearrange("p (c w) -> p c w", w=W)
            s.copy(out=xv[:, 0:4, i * 128:(i + 1) * 128],
                   in_=ps_tr[:, :].rearrange("p (c w) -> p c w", w=128))
            ins = s.copy(out=xv[:, 4:8, i * 128:(i + 1) * 128],
                         in_=ps_tr2[:, :].rearrange("p (c w) -> p c w", w=128))
            e.s.inc(ins)
        blk.scalar(s_ev)
        marks[("ev", i)] = (e.s, e.s.mark())

    return {"fm": (e.s, e.s.mark()), "stats": stats_mark}


def _emit_ln(e, ph, xn_fm, pad, after_marks, tag="ln"):
    sm = _emit_ln_stats(e, after_marks)
    return _emit_ln_apply(e, ph, xn_fm, pad, 0, NT, tag, sm)



def _emit_l0f(e, first):
    nc, blk = e.nc, e.blk
    SC = 1.0 / 127.0
    NSL = 2
    TSL = 1024
    with ExitStack() as ph:
        sb = lambda n, sh, dt: ph.enter_context(nc.sbuf_tensor(n, sh, dt))
        e.wsml = [sb(f"l0f_ws{i}", [128, 1024], BF16) for i in range(4)]
        xn = sb("l0f_xn", [128, NC_ * 2048], BF16)
        hbuf = sb("l0f_h", [128, 24 * TSL], BF16)
        wdbuf = sb("l0f_wdb", [128, 12 * 1024], BF16)
        t1 = sb("l0f_t1", [128, 512], BF16)
        t2 = [sb(f"l0f_t2{i}", [128, 512], BF16) for i in range(2)]
        ps_gu = [ph.enter_context(nc.psum_tensor(f"l0f_pgu{i}", [128, 512], F32)) for i in range(4)]
        ps_d = [ph.enter_context(nc.psum_tensor(f"l0f_pd{i}", [128, 512], F32)) for i in range(2)]

        lnm = _emit_ln(e, ph, xn, 0, {"resid": (e.v, e.v.mark())}, tag="l0f")
        _tap(e, "xn", xn[:, :], [lnm["fm"]])
        _tap(e, "mu", e.mu[:, :], [lnm["stats"]])
        _tap(e, "rstd", e.rstd[:, :], [lnm["stats"]])

        wm, mm_m, ev_m, tt_m, dn_m = {}, {}, {}, {}, {}

        def load_gu(k):
            o = k % 24
            ch = _gch(e, f"l0f_gu{k % 2}")
            def g_w(g, ch=ch):
                if ("mm", k - 2) in mm_m:
                    _need(g, mm_m[("mm", k - 2)])
                ins = g.dma_start(out=e.wsml[(k % 2) * 2][:, :],
                                  in_=e.hbm["l0f_wg"][o].rearrange("p c j -> p (c j)"))
                ch.inc(ins, 16)
                ins = g.dma_start(out=e.wsml[(k % 2) * 2 + 1][:, :],
                                  in_=e.hbm["l0f_wu"][o].rearrange("p c j -> p (c j)"))
                ch.inc(ins, 16)
            blk.gpsimd(g_w)
            wm[("w", k)] = (ch, ch.mark())

        def load_wd(cpass, guard):
            ch = _gch(e, "l0f_wd")
            def g_w(g, ch=ch):
                _need(g, guard)
                for c in range(cpass * 12, cpass * 12 + 12):
                    ins = g.dma_start(out=wdbuf[:, (c % 12) * 1024:(c % 12 + 1) * 1024],
                                      in_=e.hbm["l0f_wd"][c])
                    ch.inc(ins, 16)
            blk.gpsimd(g_w)
            wm[("wd", cpass)] = (ch, ch.mark())

        load_gu(0)
        load_gu(1)
        load_wd(0, (e.t, 0))

        for sl in range(NSL):
            for o in range(24):
                k = sl * 24 + o
                wg, wu = e.wsml[(k % 2) * 2], e.wsml[(k % 2) * 2 + 1]
                for sg in range(2):
                    j = k * 2 + sg
                    pg, pu = ps_gu[(j % 2) * 2], ps_gu[(j % 2) * 2 + 1]
                    def t_mm(t, sl=sl, o=o, sg=sg, k=k, j=j, wg=wg, wu=wu, pg=pg, pu=pu):
                        _need(t, wm[("w", k)])
                        if j == 0:
                            _need(t, lnm["fm"])
                        if ("done", j - 2) in ev_m:
                            _need(t, ev_m[("done", j - 2)])
                        if ("tt", j - 2) in tt_m:
                            _need(t, tt_m[("tt", j - 2)])
                        if sl > 0 and o == 0 and sg == 0:
                            _need(t, dn_m[("add", sl - 1, 31)])
                        col0 = sl * 1024 + sg * 512
                        for c in range(NC_):
                            t.matmul(pg[:, :], wg[:, c * 128:(c + 1) * 128],
                                     xn[:, c * 2048 + col0: c * 2048 + col0 + 512],
                                     start=(c == 0), stop=(c == 7))
                        for c in range(NC_):
                            ins = t.matmul(pu[:, :], wu[:, c * 128:(c + 1) * 128],
                                           xn[:, c * 2048 + col0: c * 2048 + col0 + 512],
                                           start=(c == 0), stop=(c == 7))
                        e.t.inc(ins)
                    blk.tensor(t_mm)
                    mm_m[("mmj", j)] = (e.t, e.t.mark())
                    if sg == 1:
                        mm_m[("mm", k)] = (e.t, e.t.mark())
                        if k + 2 < NSL * 24:
                            load_gu(k + 2)

                    def s_ev(s, j=j, pg=pg):
                        _need(s, mm_m[("mmj", j)])
                        if ("tt", j - 1) in tt_m:
                            _need(s, tt_m[("tt", j - 1)])
                        ins = s.activation(out=t1[:, :], in_=pg[:, :], func=AF.Silu, scale=SC)
                        e.s.inc(ins)
                    blk.scalar(s_ev)
                    ev_m[("done", j)] = (e.s, e.s.mark())

                    def v_tt(v, sl=sl, o=o, sg=sg, j=j, pu=pu):
                        _need(v, ev_m[("done", j)])
                        if ("tt", j - 2) in tt_m:
                            _need(v, tt_m[("tt", j - 2)])
                        if sl > 0 and o == 0 and sg == 0:
                            _need(v, dn_m[("add", sl - 1, 31)])
                        v.tensor_scalar_mul(out=t2[j % 2][:, :], in0=pu[:, :], scalar1=SC)
                        v.drain()
                        ins = v.tensor_tensor(out=hbuf[:, o * TSL + sg * 512: o * TSL + (sg + 1) * 512],
                                              in0=t1[:, :], in1=t2[j % 2][:, :], op=ALU.mult)
                        e.v.inc(ins)
                    blk.vector(v_tt)
                    tt_m[("tt", j)] = (e.v, e.v.mark())

            h_ready = tt_m[("tt", sl * 48 + 47)]
            if sl == 0:
                _tap(e, "h0", hbuf[:, :], [h_ready])
            for cpass in range(2):
                for ti in range(8):
                    for half in range(2):
                        k2 = cpass * 16 + ti * 2 + half
                        pd = ps_d[k2 % 2]
                        def t_dn(t, sl=sl, cpass=cpass, ti=ti, half=half, k2=k2, pd=pd):
                            _need(t, wm[("wd", cpass)], h_ready)
                            if ("add", sl, k2 - 2) in dn_m:
                                _need(t, dn_m[("add", sl, k2 - 2)])
                            elif sl > 0 and k2 < 2:
                                _need(t, dn_m[("add", sl - 1, 30 + k2)])
                            for ci in range(12):
                                c = cpass * 12 + ci
                                ins = t.matmul(pd[:, :],
                                               hbuf[:, c * TSL + ti * 128: c * TSL + (ti + 1) * 128],
                                               wdbuf[:, ci * 1024 + half * 512: ci * 1024 + (half + 1) * 512],
                                               start=(ci == 0), stop=(ci == 11))
                            e.t.inc(ins)
                        blk.tensor(t_dn)
                        dn_m[("mm", sl, k2)] = (e.t, e.t.mark())

                        def v_add(v, sl=sl, ti=ti, half=half, k2=k2, pd=pd):
                            _need(v, dn_m[("mm", sl, k2)])
                            t_g = sl * 8 + ti
                            dst = e.resid[:, t_g * 1024 + half * 512: t_g * 1024 + (half + 1) * 512]
                            ins = v.scalar_tensor_tensor(out=dst, in0=pd[:, :], scalar=SC,
                                                         in1=dst, op0=ALU.mult, op1=ALU.add)
                            e.v.inc(ins)
                        blk.vector(v_add)
                        dn_m[("add", sl, k2)] = (e.v, e.v.mark())
                if cpass == 0:
                    load_wd(1, dn_m[("mm", sl, 15)])
                elif sl + 1 < NSL:
                    load_wd(0, dn_m[("mm", sl, 31)])

        e.m_block_end = [dn_m[("add", NSL - 1, 31)]]
        _emit_join(e)


def _emit_join(e):
    blk = e.blk
    mt, ms, mv = (e.t, e.t.mark()), (e.s, e.s.mark()), (e.v, e.v.mark())
    mg, my = (e.g, e.g.mark()), (e.y, e.y.mark())
    def t_j(t): _need(t, ms, mv)
    def s_j(s):
        s.drain()
        _need(s, mt, mv)
    def v_j(v):
        v.drain()
        _need(v, mt, ms)
    def g_j(g): _need(g, mt, ms, mv)
    def y_j(y): _need(y, mt, ms, mv)
    blk.tensor(t_j); blk.scalar(s_j); blk.vector(v_j); blk.gpsimd(g_j); blk.sync(y_j)


def _emit_l0a(e, first):
    nc, blk = e.nc, e.blk
    PF, PB = 16, 128
    WCH = PF + 2048 + PB
    ROT_A = {0: 15, 1: 111, 2: 79, 3: 47}
    ROT_B = {0: 79, 1: 47, 2: 15, 3: 111}
    with ExitStack() as ph:
        sb = lambda n, sh, dt: ph.enter_context(nc.sbuf_tensor(n, sh, dt))
        e.wbig = [sb(f"l0a_wb{i}", [128, 4096], BF16) for i in range(2)]
        xnp = sb("l0a_xn", [128, NC_ * WCH], BF16)
        zall = sb("l0a_z", [128, 8 * 2048], BF16)
        trib = sb("l0a_tri", [128, 47], BF16)
        selg = sb("l0a_selg", [128, 512], BF16)
        sel0 = sb("l0a_sel0", [128, 512], BF16)
        oab = sb("l0a_oab", [128, 2], BF16)
        rep2 = sb("l0a_rp2", [2, 128], F32)
        inner = ExitStack()
        sbi = lambda n, sh, dt: inner.enter_context(nc.sbuf_tensor(n, sh, dt))
        qb = sbi("l0a_qb", [128, 2048], BF16)
        kb = sbi("l0a_kb", [128, 2063], BF16)
        kpb = sbi("l0a_kpb", [128, 2063], BF16)
        vfm = sbi("l0a_vfm", [128, 2063], BF16)
        rots = {}
        for hd in ("a", "b"):
            for r in (15, 47, 79, 111):
                rots[(hd, r)] = sbi(f"l0a_r{hd}{r}", [128, 17 * 64], BF16)
        ubuf = sbi("l0a_u", [128, 16 * 256], BF16)
        expb = [sbi(f"l0a_e{i}", [128, 512], BF16) for i in range(2)]
        wbf = [sbi(f"l0a_w{i}", [128, 512], BF16) for i in range(2)]
        zsl = sbi("l0a_zs", [128, 512], F32)
        dsl = sbi("l0a_ds", [2, 512], F32)

        sm = _emit_ln_stats(e, {"resid": (e.v, e.v.mark())})
        lnscope = ExitStack()
        lnm = _emit_ln_apply(e, lnscope, xnp, (PF, PB), 0, NT, "l0a", sm)
        lnscope.close()

        chk = _gch(e, "l0a_con")
        def g_consts(g):
            chk.inc(g.dma_start(out=trib[:, :], in_=e.hbm["c_l0tri"]), 16)
            chk.inc(g.dma_start(out=selg[:, :], in_=e.hbm["c_l0sel_gen"]), 16)
            chk.inc(g.dma_start(out=sel0[:, :], in_=e.hbm["c_l0sel_g0"]), 16)
            chk.inc(g.dma_start(out=oab[:, :], in_=e.hbm["c_l0ones"]), 16)
            chk.inc(g.dma_start(out=rep2[:, :], in_=e.hbm["c_l0rep"]), 16)
        blk.gpsimd(g_consts)
        m_con = (chk, chk.mark())

        wm, km, gm, dv = {}, {}, {}, {}

        def load_wp(pr):
            ch = _gch(e, f"l0a_wp{pr % 2}")
            def g_w(g, ch=ch):
                if ("wpu", pr - 2) in km:
                    _need(g, km[("wpu", pr - 2)])
                ins = g.dma_start(out=e.wbig[pr % 2][:, :],
                                  in_=e.hbm["l0a_wp"][pr].rearrange("p w c j -> p (w c j)"))
                ch.inc(ins, 16)
            blk.gpsimd(g_w)
            wm[("wp", pr)] = (ch, ch.mark())

        load_wp(0)
        load_wp(1)

        ps_a = [inner.enter_context(nc.psum_tensor(f"l0a_pa{i}", [128, 512], F32)) for i in range(3)]
        ps_pj = [ps_a[0], ps_a[1]]
        ps_r = [inner.enter_context(nc.psum_tensor(f"l0a_pr{i}", [128, 512], F32)) for i in range(2)]
        ps_d2 = [inner.enter_context(nc.psum_tensor(f"l0a_pd{i}", [128, 512], F32)) for i in range(2)]
        ps_rep = inner.enter_context(nc.psum_tensor("l0a_prp", [128, 512], F32))
        def v_zero(v):
            _need(v, lnm["fm"])
            for pa in ps_a:
                v.memset(pa[32:64, :], 0)
                v.memset(pa[96:128, :], 0)
            e.v.inc(v.drain())
        blk.vector(v_zero)
        m_zero = (e.v, e.v.mark())

        for pr in range(8):
            wp = e.wbig[pr % 2]
            for wi, dst, off in ((0, qb, 0), (1, kb, 15), (2, kpb, 15), (3, vfm, 15)):
                for sg in range(4):
                    pp = ps_pj[sg % 2]
                    def t_p(t, pr=pr, wi=wi, sg=sg, pp=pp):
                        _need(t, wm[("wp", pr)], lnm["fm"])
                        if pr == 0 and wi == 0 and sg == 0:
                            _need(t, m_zero)
                        if pr == 1 and getattr(e, "tap_guard", None):
                            _need(t, e.tap_guard)
                        key = ("pev", pr, wi, sg - 2)
                        key2 = ("pev", pr, wi - 1, 2 + sg)
                        key3 = ("pev", pr - 1, 3, 2 + sg)
                        for kk in (key, key2, key3):
                            if kk in km:
                                _need(t, km[kk])
                                break
                        if wi == 0 and sg < 2 and pr > 0:
                            _need(t, gm[("exp", pr - 1, 63)])
                        for c in range(NC_):
                            ins = t.matmul(pp[:, :], wp[:, wi * 1024 + c * 128: wi * 1024 + (c + 1) * 128],
                                           xnp[:, c * WCH + PF + sg * 512: c * WCH + PF + (sg + 1) * 512],
                                           start=(c == 0), stop=(c == 7))
                        e.t.inc(ins)
                    blk.tensor(t_p)
                    km[("pmm", pr, wi, sg)] = (e.t, e.t.mark())

                    def s_p(s, pr=pr, wi=wi, sg=sg, dst=dst, off=off, pp=pp):
                        _need(s, km[("pmm", pr, wi, sg)])
                        ins = s.activation(out=dst[:, off + sg * 512: off + (sg + 1) * 512],
                                           in_=pp[:, :], func=AF.Identity,
                                           scale=0.125 if wi == 0 else 1.0)
                        e.s.inc(ins)
                    blk.scalar(s_p)
                    km[("pev", pr, wi, sg)] = (e.s, e.s.mark())

            def v_kp0(v, pr=pr):
                if ("gdone", pr - 1, 63) in gm:
                    _need(v, gm[("gdone", pr - 1, 63)])
                v.memset(kb[:, 0:15], 0)
                v.memset(kpb[:, 0:15], 0)
                ins = v.memset(vfm[:, 0:15], 0)
                e.v.inc(ins)
            blk.vector(v_kp0)
            km[("kp0", pr)] = (e.v, e.v.mark())

            for tau in range(17):
                pp = ps_pj[tau % 2]
                def t_v(t, pr=pr, tau=tau, pp=pp):
                    _need(t, wm[("wp", pr)], lnm["fm"])
                    key = ("vev", pr, tau - 2)
                    if key in km:
                        _need(t, km[key])
                    elif tau < 2:
                        _need(t, km[("pev", pr, 3, 2 + tau)])
                    for c in range(NC_):
                        ins = t.matmul(pp[:, 0:128],
                                       xnp[:, c * WCH + PF + 128 * tau - 15: c * WCH + PF + 128 * tau + 113],
                                       wp[:, 3 * 1024 + c * 128: 3 * 1024 + (c + 1) * 128],
                                       start=(c == 0), stop=(c == 7))
                    e.t.inc(ins)
                blk.tensor(t_v)
                km[("vmm", pr, tau)] = (e.t, e.t.mark())

                def s_v(s, pr=pr, tau=tau, pp=pp):
                    _need(s, km[("vmm", pr, tau)])
                    if ("rotdma", pr - 1) in km and tau == 0:
                        _need(s, km[("rotdma", pr - 1)])
                    s.copy(out=rots[("a", 15)][:, tau * 64:(tau + 1) * 64], in_=pp[:, 0:64])
                    ins = s.copy(out=rots[("b", 15)][:, tau * 64:(tau + 1) * 64], in_=pp[:, 64:128])
                    e.s.inc(ins)
                blk.scalar(s_v)
                km[("vev", pr, tau)] = (e.s, e.s.mark())
            m_v15 = (e.s, e.s.mark())

            def y_rot(y, pr=pr):
                _need(y, m_v15)
                if ("gdone", pr - 1, 63) in gm:
                    _need(y, gm[("gdone", pr - 1, 63)])
                for hd in ("a", "b"):
                    sa = rots[(hd, 15)]
                    for kk_, rdst in ((1, 47), (2, 79), (3, 111)):
                        da = rots[(hd, rdst)]
                        sh = 32 * kk_
                        ins = y.dma_start(out=da[sh:128, :], in_=sa[0:128 - sh, :])
                        e.y.inc(ins, 16)
                        ins = y.dma_start(out=da[0:sh, 64:17 * 64], in_=sa[128 - sh:128, 0:16 * 64])
                        e.y.inc(ins, 16)
            blk.sync(y_rot)
            km[("rotdma0", pr)] = (e.y, e.y.mark())

            def v_rot0(v, pr=pr):
                _need(v, km[("rotdma0", pr)])
                for hd in ("a", "b"):
                    for kk_, rdst in ((1, 47), (2, 79), (3, 111)):
                        v.memset(rots[(hd, rdst)][0:32 * kk_, 0:64], 0)
                e.v.inc(v.drain())
            blk.vector(v_rot0)
            km[("rotdma", pr)] = (e.v, e.v.mark())

            def emit_u(pr, sgq, after_g):
                def v_u(v, pr=pr, sgq=sgq, after_g=after_g):
                    _need(v, km[("pev", pr, 0, min(3, (sgq * 256 + 255) // 512))],
                          km[("pev", pr, 1, min(3, (sgq * 256 + 255) // 512))],
                          km[("kp0", pr)])
                    if ("amm", pr, after_g) in gm:
                        _need(v, gm[("amm", pr, after_g)])
                    elif pr > 0 and sgq == 0:
                        _need(v, gm[("amm", pr - 1, 63)])
                    for o1 in range(16):
                        ins = v.tensor_tensor(
                            out=ubuf[:, o1 * 256:(o1 + 1) * 256],
                            in0=qb[:, sgq * 256:(sgq + 1) * 256],
                            in1=kb[:, 15 - o1 + sgq * 256: 15 - o1 + (sgq + 1) * 256],
                            op=ALU.mult)
                    e.g.inc(ins)
                blk.gpsimd(v_u)
                gm[("u", pr, sgq)] = (e.g, e.g.mark())

            def emit_a(pr, g):
                if g % 8 == 0:
                    emit_u(pr, g // 8, g - 1)
                pa = ps_a[g % 3]
                def t_a(t, pr=pr, g=g, pa=pa):
                    _need(t, gm[("u", pr, g // 8)], m_zero, m_con)
                    if g < 2:
                        _need(t, km[("vev", pr, 16)])
                    if ("exp", pr, g - 3) in gm:
                        _need(t, gm[("exp", pr, g - 3)])
                    elif ("exp", pr - 1, 61 + g) in gm:
                        _need(t, gm[("exp", pr - 1, 61 + g)])
                    soff = (g % 8) * 32
                    rhs_a = ubuf[0:64, :].rearrange("p (o s) -> p o s", o=16)[:, :, soff:soff + 32]
                    rhs_b = ubuf[64:128, :].rearrange("p (o s) -> p o s", o=16)[:, :, soff:soff + 32]
                    sel = sel0 if g == 0 else selg
                    t.matmul(pa[0:47, :], kpb[0:64, 32 * g:32 * g + 47], rhs_a,
                             start=True, stop=False)
                    t.matmul(pa[0:47, :], trib[0:47, :], sel[0:47, :],
                             start=False, stop=True)
                    t.matmul(pa[64:111, :], kpb[64:128, 32 * g:32 * g + 47], rhs_b,
                             start=True, stop=False, tile_position=(64, 64))
                    ins = t.matmul(pa[64:111, :], trib[64:111, :], sel[64:111, :],
                                   start=False, stop=True, tile_position=(64, 64))
                    e.t.inc(ins)
                blk.tensor(t_a)
                gm[("amm", pr, g)] = (e.t, e.t.mark())

            def emit_exp_mask(pr, g):
                pa = ps_a[g % 3]
                def s_e(s, pr=pr, g=g, pa=pa):
                    _need(s, gm[("amm", pr, g)])
                    if ("rmm", pr, g - 2) in gm:
                        _need(s, gm[("rmm", pr, g - 2)])
                    elif ("rmm", pr - 1, 62 + g) in gm:
                        _need(s, gm[("rmm", pr - 1, 62 + g)])
                    ins = s.activation(out=expb[g % 2][:, :], in_=pa[:, :], func=AF.Exp)
                    e.s.inc(ins)
                blk.scalar(s_e)
                gm[("exp", pr, g)] = (e.s, e.s.mark())

            def emit_rgrp(pr, g):
                psr = ps_r[g % 2]
                def t_r(t, pr=pr, g=g, psr=psr):
                    _need(t, gm[("exp", pr, g)], km[("rotdma", pr)])
                    if ("wtt", pr, g - 2) in gm:
                        _need(t, gm[("wtt", pr, g - 2)])
                    elif ("wtt", pr - 1, 62 + g) in gm:
                        _need(t, gm[("wtt", pr - 1, 62 + g)])
                    if ("dred", pr, g - 2) in gm:
                        _need(t, gm[("dred", pr, g - 2)])
                    elif ("dred", pr - 1, 62 + g) in gm:
                        _need(t, gm[("dred", pr - 1, 62 + g)])
                    me = expb[g % 2]
                    w = 32 * g - 15
                    ra = rots[("a", ROT_A[g % 4])]
                    rb = rots[("b", ROT_B[g % 4])]
                    ta = (w - 0 + ROT_A[g % 4]) // 128
                    tb = (w - 64 + ROT_B[g % 4]) // 128
                    t.matmul(psr[0:64, :], ra[0:47, ta * 64:(ta + 1) * 64], me[0:47, :],
                             start=True, stop=True)
                    t.matmul(psr[64:128, :], rb[64:111, tb * 64:(tb + 1) * 64], me[64:111, :],
                             start=True, stop=True, tile_position=(64, 64))
                    ins = t.matmul(ps_d2[g % 2][0:2, :], oab[:, :], me[:, :],
                                   start=True, stop=True)
                    e.t.inc(ins)
                blk.tensor(t_r)
                gm[("rmm", pr, g)] = (e.t, e.t.mark())

                def v_wtt(v, pr=pr, g=g, psr=psr):
                    _need(v, gm[("rmm", pr, g)])
                    if ("wred", pr, g - 2) in gm:
                        _need(v, gm[("wred", pr, g - 2)])
                    elif ("wred", pr - 1, 62 + g) in gm:
                        _need(v, gm[("wred", pr - 1, 62 + g)])
                    ins = v.tensor_tensor(out=wbf[g % 2][:, :].rearrange("p (s o) -> p o s", o=16),
                                    in0=psr[:, :].rearrange("p (o s) -> p o s", o=16),
                                    in1=_vfm_shift_view(vfm, g), op=ALU.mult)
                    e.v.inc(ins)
                blk.vector(v_wtt)
                gm[("wtt", pr, g)] = (e.v, e.v.mark())

                def v_dred(v, pr=pr, g=g):
                    _need(v, gm[("rmm", pr, g)])
                    if g % 16 == 0:
                        if ("dinv", pr, g // 16 - 1) in dv:
                            _need(v, dv[("dinv", pr, g // 16 - 1)])
                        elif pr > 0 and ("dinv", pr - 1, 3) in dv:
                            _need(v, dv[("dinv", pr - 1, 3)])
                    ins = v.tensor_reduce(out=dsl[0:2, (g % 16) * 32:(g % 16 + 1) * 32],
                                    in_=ps_d2[g % 2][0:2, :].rearrange("p (o s) -> p s o", o=16),
                                    axis=AX.X, op=ALU.add)
                    e.v.inc(ins)
                blk.vector(v_dred)
                gm[("dred", pr, g)] = (e.v, e.v.mark())
                gm[("gdone", pr, g)] = gm[("dred", pr, g)]

                def v_wred(v, pr=pr, g=g):
                    _need(v, gm[("wtt", pr, g)])
                    if g % 16 == 0:
                        if ("ztt", pr, g // 16 - 1) in dv:
                            _need(v, dv[("ztt", pr, g // 16 - 1)])
                        elif g == 0 and pr > 0:
                            _need(v, dv[("ztt", pr - 1, 3)])
                    ins = v.tensor_reduce(out=zsl[:, (g % 16) * 32:(g % 16 + 1) * 32],
                                    in_=wbf[g % 2][:, :].rearrange("p (s o) -> p s o", o=16),
                                    axis=AX.X, op=ALU.add)
                    e.v.inc(ins)
                blk.vector(v_wred)
                gm[("wred", pr, g)] = (e.v, e.v.mark())

                if g % 16 == 15:
                    sgz = g // 16
                    def v_div(v, pr=pr, sgz=sgz):
                        _need(v, gm[("wred", pr, sgz * 16 + 15)], gm[("dred", pr, sgz * 16 + 15)])
                        if ("rep", pr, sgz - 1) in dv:
                            _need(v, dv[("rep", pr, sgz - 1)])
                        elif pr > 0 and sgz == 0:
                            _need(v, dv[("rep", pr - 1, 3)])
                        v.drain()
                        ins = v.reciprocal(out=dsl[0:2, :], in_=dsl[0:2, :])
                        e.v.inc(ins)
                    blk.vector(v_div)
                    dv[("dinv", pr, sgz)] = (e.v, e.v.mark())

                    def t_rep(t, pr=pr, sgz=sgz):
                        _need(t, dv[("dinv", pr, sgz)], m_con)
                        if ("ztt", pr, sgz - 1) in dv:
                            _need(t, dv[("ztt", pr, sgz - 1)])
                        elif pr > 0 and sgz == 0:
                            _need(t, dv[("ztt", pr - 1, 3)])
                        ins = t.matmul(ps_rep[:, :], rep2[:, :], dsl[0:2, :],
                                       start=True, stop=True)
                        e.t.inc(ins)
                    blk.tensor(t_rep)
                    dv[("rep", pr, sgz)] = (e.t, e.t.mark())

                    def v_z(v, pr=pr, sgz=sgz):
                        _need(v, dv[("rep", pr, sgz)])
                        ins = v.tensor_tensor(out=zall[:, pr * 2048 + sgz * 512: pr * 2048 + (sgz + 1) * 512],
                                              in0=zsl[:, :], in1=ps_rep[:, :], op=ALU.mult)
                        e.v.inc(ins)
                    blk.vector(v_z)
                    dv[("ztt", pr, sgz)] = (e.v, e.v.mark())

            emit_a(pr, 0)
            emit_a(pr, 1)
            for g in range(64):
                emit_exp_mask(pr, g)
                emit_rgrp(pr, g)
                if g + 2 < 64:
                    emit_a(pr, g + 2)

            km[("wpu", pr)] = km[("vmm", pr, 16)]
            if pr + 2 < 8:
                load_wp(pr + 2)
            if pr == 0 and "q0" in e.want_taps:
                _tap(e, "q0", qb[:, :], [gm[("gdone", 0, 63)]])
                _tap(e, "kp0", kpb[:, :], [gm[("gdone", 0, 63)]])
                _tap(e, "k0", kb[:, :], [gm[("gdone", 0, 63)]])
                _tap(e, "r15a", rots[("a", 15)][:, :], [gm[("gdone", 0, 63)]])
                e.tap_guard = (e.y, e.y.mark())

        m_z_all = (e.v, e.v.mark())
        _tap(e, "zall", zall[:, :], [m_z_all])
        _tap(e, "q7", qb[:, :], [m_z_all])
        _tap(e, "kp7", kpb[:, :], [m_z_all])
        inner.close()

        chwo = _gch(e, "l0a_wo")
        def g_wo(g):
            _need(g, km[("wpu", 7)], km[("vmm", 7, 16)])
            for c in range(8):
                ins = g.dma_start(out=e.wbig[c // 4][:, (c % 4) * 1024:(c % 4 + 1) * 1024],
                                  in_=e.hbm["l0a_wo"][c])
                chwo.inc(ins, 16)
        blk.gpsimd(g_wo)
        m_wo = (chwo, chwo.mark())

        with ExitStack() as eph:
            gam = eph.enter_context(nc.sbuf_tensor("l0a_gam", [128, 1024], F32))
            bet = eph.enter_context(nc.sbuf_tensor("l0a_bet", [128, 1024], F32))
            xtm = eph.enter_context(nc.sbuf_tensor("l0a_xtm", [128, 1024], BF16))
            hb = eph.enter_context(nc.sbuf_tensor("l0a_hb", [128, 1024], F32))
            tb = eph.enter_context(nc.sbuf_tensor("l0a_tb", [128, 1024], F32))
            bn2 = eph.enter_context(nc.sbuf_tensor("l0a_bn2", [128, 12], F32))
            st2 = eph.enter_context(nc.sbuf_tensor("l0a_st2", [128, 4], F32))
            ps_o = [eph.enter_context(nc.psum_tensor(f"l0a_po{i}", [128, 512], F32)) for i in range(2)]

            chgb = _gch(e, "l0a_gb")
            def g_gb(g):
                _need(g, m_z_all)
                chgb.inc(g.dma_start(out=gam[:, :], in_=e.hbm["l0a_gamma"]), 16)
                chgb.inc(g.dma_start(out=bet[:, :], in_=e.hbm["l0a_beta"]), 16)
            blk.gpsimd(g_gb)
            m_gb = (chgb, chgb.mark())

            em = {}
            for tau in range(NT):
                for half in range(2):
                    kk = tau * 2 + half
                    po = ps_o[kk % 2]
                    def t_o(t, tau=tau, half=half, kk=kk, po=po):
                        _need(t, m_wo, m_z_all)
                        if ("hst", tau - 1) in em:
                            _need(t, em[("hst", tau - 1)])
                        for pr in range(8):
                            wo = e.wbig[pr // 4][:, (pr % 4) * 1024 + half * 512:(pr % 4) * 1024 + (half + 1) * 512]
                            ins = t.matmul(po[:, :],
                                           zall[:, pr * 2048 + tau * 128: pr * 2048 + (tau + 1) * 128],
                                           wo, start=(pr == 0), stop=(pr == 7))
                        e.t.inc(ins)
                    blk.tensor(t_o)
                    em[("omm", kk)] = (e.t, e.t.mark())

                def v_x(v, tau=tau):
                    if ("ln2", tau - 1) in em:
                        _need(v, em[("ln2", tau - 1)])
                    ins = v.tensor_scalar(out=xtm[:, :], in0=e.resid[:, tau * 1024:(tau + 1) * 1024],
                                          scalar1=e.rstd[:, tau:tau + 1],
                                          scalar2=e.lnb[:, tau:tau + 1],
                                          op0=ALU.mult, op1=ALU.add)
                    e.v.inc(ins)
                    v.drain()
                blk.vector(v_x)
                em[("xtm", tau)] = (e.v, e.v.mark())

                def v_h(v, tau=tau):
                    _need(v, em[("omm", tau * 2)], em[("omm", tau * 2 + 1)], em[("xtm", tau)])
                    if ("radd", tau - 1) in em:
                        _need(v, em[("radd", tau - 1)])
                    v.tensor_tensor(out=hb[:, 0:512], in0=ps_o[0][:, :],
                                    in1=xtm[:, 0:512], op=ALU.add)
                    v.tensor_tensor(out=hb[:, 512:1024], in0=ps_o[1][:, :],
                                    in1=xtm[:, 512:1024], op=ALU.add)
                    v.drain()
                    v.bn_stats(bn2[:, 0:6], hb[:, 0:512])
                    v.bn_stats(bn2[:, 6:12], hb[:, 512:1024])
                    v.drain()
                    bnv = bn2[:, :].rearrange("p (g k) -> p g k", k=3)
                    v.tensor_reduce(out=st2[:, 0:1], in_=bnv[:, :, 1], axis=AX.X, op=ALU.add)
                    v.tensor_reduce(out=st2[:, 1:2], in_=bnv[:, :, 2], axis=AX.X, op=ALU.add)
                    v.tensor_tensor(out=bnv[:, :, 0], in0=bnv[:, :, 1], in1=bnv[:, :, 1], op=ALU.mult)
                    v.drain()
                    v.tensor_reduce(out=st2[:, 2:3], in_=bnv[:, :, 0], axis=AX.X, op=ALU.add)
                    v.tensor_scalar_mul(out=st2[:, 0:1], in0=st2[:, 0:1], scalar1=0.25)
                    v.drain()
                    v.tensor_scalar_mul(out=st2[:, 2:3], in0=st2[:, 2:3], scalar1=0.25)
                    v.drain()
                    v.scalar_tensor_tensor(out=st2[:, 2:3], in0=st2[:, 1:2], scalar=1.0 / 1024,
                                           in1=st2[:, 2:3], op0=ALU.mult, op1=ALU.add)
                    v.drain()
                    v.tensor_tensor(out=st2[:, 1:2], in0=st2[:, 0:1], in1=st2[:, 0:1], op=ALU.mult)
                    v.drain()
                    v.tensor_tensor(out=st2[:, 2:3], in0=st2[:, 2:3], in1=st2[:, 1:2], op=ALU.subtract)
                    e.v.inc(v.drain())
                blk.vector(v_h)
                em[("hst", tau)] = (e.v, e.v.mark())

                def s_r2(s, tau=tau):
                    _need(s, em[("hst", tau)])
                    s.activation(out=st2[:, 2:3], in_=st2[:, 2:3], func=AF.Ln,
                                 bias=e.c_misc[:, 0:1], scale=1.0)
                    s.drain()
                    s.activation(out=st2[:, 2:3], in_=st2[:, 2:3], func=AF.Exp, scale=-0.5)
                    s.drain()
                    e.s.inc(s.drain())
                blk.scalar(s_r2)
                em[("r2", tau)] = (e.s, e.s.mark())

                def v_b2(v, tau=tau):
                    _need(v, em[("r2", tau)])
                    v.scalar_tensor_tensor(out=st2[:, 3:4], in0=st2[:, 0:1], scalar=-1.0,
                                           in1=st2[:, 2:3], op0=ALU.mult, op1=ALU.mult)
                    e.v.inc(v.drain())
                blk.vector(v_b2)
                em[("b2", tau)] = (e.v, e.v.mark())

                def v_ln2(v, tau=tau):
                    _need(v, em[("b2", tau)], m_gb)
                    ins = v.tensor_scalar(out=hb[:, :], in0=hb[:, :],
                                          scalar1=st2[:, 2:3], scalar2=st2[:, 3:4],
                                          op0=ALU.mult, op1=ALU.add)
                    e.v.inc(ins)
                    v.drain()
                blk.vector(v_ln2)
                em[("ln2", tau)] = (e.v, e.v.mark())

                def v_r(v, tau=tau):
                    _need(v, em[("ln2", tau)])
                    v.tensor_tensor(out=tb[:, :], in0=hb[:, :], in1=gam[:, :], op=ALU.mult)
                    v.drain()
                    v.tensor_tensor(out=tb[:, :], in0=tb[:, :], in1=bet[:, :], op=ALU.add)
                    v.drain()
                    ins = v.tensor_tensor(out=e.resid[:, tau * 1024:(tau + 1) * 1024],
                                          in0=e.resid[:, tau * 1024:(tau + 1) * 1024],
                                          in1=tb[:, :], op=ALU.add)
                    e.v.inc(ins)
                    v.drain()
                blk.vector(v_r)
                em[("radd", tau)] = (e.v, e.v.mark())

        e.m_block_end = [em[("radd", NT - 1)]]
        _emit_join(e)


def _emit_l1a(e, first):
    nc, blk = e.nc, e.blk
    with ExitStack() as ph:
        sb = lambda n, sh, dt: ph.enter_context(nc.sbuf_tensor(n, sh, dt))
        e.wsml = [sb(f"l1a_ws{i}", [128, 1024], BF16) for i in range(4)]
        e.wbig = [sb(f"l1a_wb{i}", [128, 4096], BF16) for i in range(2)]
        qbuf = sb("l1a_q", [128, NC_ * 2048], BF16)
        kbuf = sb("l1a_k", [128, NC_ * 2048], BF16)
        vtm = sb("l1a_v", [128, NT * 1040], BF16)
        b8 = sb("l1a_b8", [128, 16], F32)
        msk = sb("l1a_msk", [128, 144], BF16)

        sm = _emit_ln_stats(e, {"resid": (e.v, e.v.mark())})

        wm, pm, am, om = {}, {}, {}, {}

        chp = _gch(e, "l1a_pre")
        def g_pre(g):
            chp.inc(g.dma_start(out=b8[:, :], in_=e.hbm["l1a_bqk"]), 16)
            chp.inc(g.dma_start(out=msk[:, :], in_=e.hbm["c_l1mask"]), 16)
        blk.gpsimd(g_pre)
        m_pre = (chp, chp.mark())

        def v_b8(v):
            _need(v, m_pre)
            v.tensor_scalar_mul(out=b8[:, 0:8], in0=b8[:, 0:8], scalar1=0.125)
            e.v.inc(v.drain())
        blk.vector(v_b8)
        m_b8 = (e.v, e.v.mark())

        def v_ones(v):
            ins = v.memset(vtm[:, :].rearrange("p (t h c) -> p t h c", h=16, c=65)[:, :, :, 64:65], 1)
            e.v.inc(ins)
        blk.vector(v_ones)
        m_ones = (e.v, e.v.mark())

        def load_wqk(sg, o):
            k = sg * 16 + o
            ch = _gch(e, f"l1a_wqk{k % 2}")
            def g_w(g, ch=ch):
                if ("wqku", k - 2) in pm:
                    _need(g, pm[("wqku", k - 2)])
                ins = g.dma_start(out=e.wsml[(k % 2) * 2][:, :],
                                  in_=e.hbm["l1a_wqk"][o].rearrange("p c j -> p (c j)"))
                ch.inc(ins, 16)
            blk.gpsimd(g_w)
            wm[("wqk", k)] = (ch, ch.mark())

        chv = _gch(e, "l1a_wv")
        def g_wv(g):
            for c in range(8):
                ins = g.dma_start(out=e.wbig[c // 4][:, (c % 4) * 1024:(c % 4 + 1) * 1024],
                                  in_=e.hbm["l1a_wv"][c])
                chv.inc(ins, 16)
        blk.gpsimd(g_wv)
        m_wv = (chv, chv.mark())

        projscope = ExitStack()
        ps_p = [projscope.enter_context(nc.psum_tensor(f"l1a_pp{i}", [128, 512], F32)) for i in range(2)]
        ps_v = [projscope.enter_context(nc.psum_tensor(f"l1a_pv{i}", [128, 1024], F32)) for i in range(2)]

        load_wqk(0, 0)
        load_wqk(0, 1)
        bv_done = {}
        for sg in range(4):
            with ExitStack() as sph:
                xn = sph.enter_context(nc.sbuf_tensor(f"l1a_xn{sg}", [128, NC_ * 512], BF16))
                lnm = _emit_ln_apply(e, sph, xn, 0, sg * 4, 4, f"l1a{sg}", sm)
                for o in range(16):
                    dstbuf = qbuf if o < 8 else kbuf
                    oc = o % 8
                    pp = ps_p[o % 2]
                    kq = sg * 16 + o
                    def t_qk(t, sg=sg, o=o, kq=kq, pp=pp):
                        _need(t, wm[("wqk", kq)], lnm["fm"])
                        if ("qkev", sg, o - 2) in pm:
                            _need(t, pm[("qkev", sg, o - 2)])
                        elif sg > 0 and o < 2:
                            _need(t, pm[("qkev", sg - 1, 14 + o)])
                        wq = e.wsml[(kq % 2) * 2]
                        for c in range(NC_):
                            ins = t.matmul(pp[:, :], wq[:, c * 128:(c + 1) * 128],
                                           xn[:, c * 512:(c + 1) * 512],
                                           start=(c == 0), stop=(c == 7))
                        e.t.inc(ins)
                    blk.tensor(t_qk)
                    pm[("qkmm", sg, o)] = (e.t, e.t.mark())
                    pm[("wqku", kq)] = (e.t, e.t.mark())
                    if kq + 2 < 64:
                        load_wqk((kq + 2) // 16, (kq + 2) % 16)

                    def s_qk(s, sg=sg, o=o, oc=oc, dstbuf=dstbuf, pp=pp):
                        _need(s, pm[("qkmm", sg, o)], m_b8)
                        ins = s.activation(out=dstbuf[:, oc * 2048 + sg * 512: oc * 2048 + (sg + 1) * 512],
                                           in_=pp[:, :], func=AF.Identity,
                                           scale=0.125 if o < 8 else 1.0,
                                           bias=b8[:, o:o + 1])
                        e.s.inc(ins)
                    blk.scalar(s_qk)
                    pm[("qkev", sg, o)] = (e.s, e.s.mark())

                for tt_i in range(4):
                    tau = sg * 4 + tt_i
                    pv = ps_v[tt_i % 2]
                    def t_v(t, sg=sg, tt_i=tt_i, tau=tau, pv=pv):
                        _need(t, m_wv, lnm["fm"])
                        if ("vev", sg, tt_i - 2) in pm:
                            _need(t, pm[("vev", sg, tt_i - 2)])
                        elif sg > 0 and tt_i < 2:
                            _need(t, pm[("vev", sg - 1, 2 + tt_i)])
                        for c in range(NC_):
                            wv = e.wbig[c // 4][:, (c % 4) * 1024:(c % 4 + 1) * 1024]
                            t.matmul(pv[:, 0:512], xn[:, c * 512 + tt_i * 128: c * 512 + (tt_i + 1) * 128],
                                     wv[:, 0:512], start=(c == 0), stop=(c == 7))
                        for c in range(NC_):
                            wv = e.wbig[c // 4][:, (c % 4) * 1024:(c % 4 + 1) * 1024]
                            ins = t.matmul(pv[:, 512:1024], xn[:, c * 512 + tt_i * 128: c * 512 + (tt_i + 1) * 128],
                                           wv[:, 512:1024], start=(c == 0), stop=(c == 7))
                        e.t.inc(ins)
                    blk.tensor(t_v)
                    pm[("vmm", sg, tt_i)] = (e.t, e.t.mark())

                    def s_v(s, tau=tau, sg=sg, tt_i=tt_i, pv=pv):
                        _need(s, pm[("vmm", sg, tt_i)], m_ones)
                        vv = vtm[:, tau * 1040:(tau + 1) * 1040].rearrange("p (h c) -> p h c", c=65)
                        ins = s.copy(out=vv[:, :, 0:64],
                                     in_=pv[:, :].rearrange("p (h c) -> p h c", c=64))
                        e.s.inc(ins)
                    blk.scalar(s_v)
                    pm[("vev", sg, tt_i)] = (e.s, e.s.mark())
        pm[("wqku_all",)] = pm[("qkev", 3, 15)]
        projscope.close()

        with ExitStack() as bvph:
            bvr = bvph.enter_context(nc.sbuf_tensor("l1a_bvr", [128, 1040], F32))
            chb = _gch(e, "l1a_bv")
            def g_bv(g):
                _need(g, pm[("vev", 3, 3)])
                chb.inc(g.dma_start(out=bvr[:, :], in_=e.hbm["l1a_bv"]), 16)
            blk.gpsimd(g_bv)
            m_bv = (chb, chb.mark())
            for tau in range(NT):
                def v_bv(v, tau=tau):
                    _need(v, m_bv, pm[("vev", tau // 4, tau % 4)])
                    ins = v.tensor_tensor(out=vtm[:, tau * 1040:(tau + 1) * 1040],
                                          in0=vtm[:, tau * 1040:(tau + 1) * 1040],
                                          in1=bvr[:, :], op=ALU.add)
                    e.v.inc(ins)
                blk.vector(v_bv)
                bv_done[tau] = (e.v, e.v.mark())
            m_v_ready = (e.v, e.v.mark())

            cho = _gch(e, "l1a_wo")
            def g_wo(g):
                _need(g, pm[("vmm", 3, 3)])
                for c in range(8):
                    ins = g.dma_start(out=e.wbig[c // 4][:, (c % 4) * 1024:(c % 4 + 1) * 1024],
                                      in_=e.hbm["l1a_wo"][c])
                    cho.inc(ins, 16)
            blk.gpsimd(g_wo)
            m_wo = (cho, cho.mark())

        with ExitStack() as aph:
            pbuf = aph.enter_context(nc.sbuf_tensor("l1a_pt", [128, 3 * 1024], BF16))
            cst = aph.enter_context(nc.sbuf_tensor("l1a_cst", [128, 512], BF16))
            stg = aph.enter_context(nc.sbuf_tensor("l1a_stg", [64, 512], BF16))
            ctxb = aph.enter_context(nc.sbuf_tensor("l1a_ctx", [128, 8 * 512], BF16))
            rcp = aph.enter_context(nc.sbuf_tensor("l1a_rcp", [128, 512], F32))
            conso = aph.enter_context(nc.sbuf_tensor("l1a_on", [128, 64], F32))
            ps_s2 = [aph.enter_context(nc.psum_tensor(f"l1a_ps{i}", [128, 1024], F32)) for i in range(2)]
            ps_c = [aph.enter_context(nc.psum_tensor(f"l1a_pc{i}", [128, 512], F32)) for i in range(2)]
            ps_r = aph.enter_context(nc.psum_tensor("l1a_pr", [128, 512], F32))

            chc = _gch(e, "l1a_on")
            def g_bo(g):
                _need(g, m_v_ready)
                chc.inc(g.dma_start(out=conso[:, :], in_=e.hbm["c_ones64"]), 16)
            blk.gpsimd(g_bo)
            m_bo = (chc, chc.mark())

            with ExitStack() as boph:
                bor = boph.enter_context(nc.sbuf_tensor("l1a_bor", [128, 1024], BF16))
                chb2 = _gch(e, "l1a_bo")
                def g_bo2(g):
                    _need(g, m_v_ready)
                    chb2.inc(g.dma_start(out=bor[:, :], in_=e.hbm["l1a_bo"]), 16)
                blk.gpsimd(g_bo2)
                m_bo2 = (chb2, chb2.mark())
                for tau in range(NT):
                    def v_bo(v, tau=tau):
                        _need(v, m_bo2)
                        if tau == 0:
                            _need(v, sm, pm[("wqku_all",)], pm[("vev", 3, 3)])
                        ins = v.tensor_tensor(out=e.resid[:, tau * 1024:(tau + 1) * 1024],
                                              in0=e.resid[:, tau * 1024:(tau + 1) * 1024],
                                              in1=bor[:, :], op=ALU.add)
                        e.v.inc(ins)
                    blk.vector(v_bo)
                m_bo_add = (e.v, e.v.mark())

            for sg in range(4):
                q0 = sg * 512
                for h in range(16):
                    hrow = (h % 2) * 64
                    hcolq = (h // 2) * 2048

                    def scp(t, P, sg=sg, h=h, hrow=hrow, hcolq=hcolq):
                        _need(t, pm[("wqku_all",)])
                        if sg == 0 and h == 0 and P < 2:
                            _need(t, pm[("vev", 3, 3)])
                        if ("exp", sg, h, P - 2) in am:
                            _need(t, am[("exp", sg, h, P - 2)])
                        elif P < 2 and ("exp", sg, h - 1, 6 + P) in am:
                            _need(t, am[("exp", sg, h - 1, 6 + P)])

                        if P < 2 and h == 0 and sg > 0:
                            _need(t, om[("oev", sg - 1, 7)])
                        for half in range(2):
                            kt = 2 * P + half
                            ins = t.matmul(ps_s2[P % 2][:, half * 512:(half + 1) * 512],
                                           kbuf[hrow:hrow + 64, hcolq + kt * 128: hcolq + (kt + 1) * 128],
                                           qbuf[hrow:hrow + 64, hcolq + q0: hcolq + q0 + 512],
                                           start=True, stop=True)
                        e.t.inc(ins)
                        am[("sc", sg, h, P)] = (e.t, e.t.mark())

                    def mskp(v, P, sg=sg, h=h):
                        any_ = False
                        ins = None
                        for half in range(2):
                            kt = 2 * P + half
                            k0 = kt * 128
                            qlo, qhi = max(k0, q0), min(k0 + 144, q0 + 512)
                            if qlo < qhi:
                                if not any_:
                                    _need(v, am[("sc", sg, h, P)], m_pre)
                                    any_ = True
                                ins = v.tensor_tensor(
                                    out=ps_s2[P % 2][:, half * 512 + qlo - q0: half * 512 + qhi - q0],
                                    in0=ps_s2[P % 2][:, half * 512 + qlo - q0: half * 512 + qhi - q0],
                                    in1=msk[:, qlo - k0:qhi - k0], op=ALU.add)
                        if any_:
                            e.v.inc(ins)
                            am[("msk", sg, h, P)] = (e.v, e.v.mark())

                    def expp(s, P, sg=sg, h=h):
                        _need(s, am.get(("msk", sg, h, P), am[("sc", sg, h, P)]))
                        if ("pv", sg, h, P - 3) in am:
                            _need(s, am[("pv", sg, h, P - 3)])
                        elif P < 3 and ("pv", sg, h - 1, 5 + P) in am:
                            _need(s, am[("pv", sg, h - 1, 5 + P)])
                        elif P < 3 and h == 0 and sg > 0 and ("pv", sg - 1, 15, 5 + P) in am:
                            _need(s, am[("pv", sg - 1, 15, 5 + P)])
                        ins = s.activation(out=pbuf[:, (P % 3) * 1024:(P % 3 + 1) * 1024],
                                           in_=ps_s2[P % 2][:, :], func=AF.Exp)
                        e.s.inc(ins)
                        am[("exp", sg, h, P)] = (e.s, e.s.mark())

                    def pvp(t, P, sg=sg, h=h):
                        psc = ps_c[h % 2]
                        _need(t, am[("exp", sg, h, P)], m_v_ready)
                        if P == 0:
                            if ("ctt", sg, h - 2) in am:
                                _need(t, am[("ctt", sg, h - 2)])
                            elif sg > 0 and h < 2:
                                _need(t, am[("ctt", sg - 1, 14 + h)])
                        for half in range(2):
                            kt = 2 * P + half
                            ins = t.matmul(psc[0:65, :],
                                           vtm[:, kt * 1040 + h * 65: kt * 1040 + (h + 1) * 65],
                                           pbuf[:, (P % 3) * 1024 + half * 512: (P % 3) * 1024 + (half + 1) * 512],
                                           start=(kt == 0), stop=(kt == 15))
                        e.t.inc(ins)
                        am[("pv", sg, h, P)] = (e.t, e.t.mark())

                    for P in range(8):
                        blk.tensor(lambda t, P=P: scp(t, P))
                        blk.vector(lambda v, P=P: mskp(v, P))
                        blk.scalar(lambda s, P=P: expp(s, P))
                        if P >= 2:
                            blk.tensor(lambda t, P=P: pvp(t, P - 2))
                    blk.tensor(lambda t: pvp(t, 6))
                    blk.tensor(lambda t: pvp(t, 7))

                    def v_rcp(v, sg=sg, h=h, psc=ps_c[h % 2]):
                        _need(v, am[("pv", sg, h, 7)])
                        if ("rep", sg, h - 1) in am:
                            _need(v, am[("rep", sg, h - 1)])
                        elif sg > 0 and h == 0:
                            _need(v, am[("rep", sg - 1, 15)])
                        ins = v.reciprocal(out=rcp[64:65, :], in_=psc[64:65, :])
                        e.v.inc(ins)
                    blk.vector(v_rcp)
                    am[("rcp", sg, h)] = (e.v, e.v.mark())

                    def v_cst(v, sg=sg, h=h, psc=ps_c[h % 2]):
                        _need(v, am[("pv", sg, h, 7)])
                        if ("ctt", sg, h - 1) in am:
                            _need(v, am[("ctt", sg, h - 1)])
                        elif sg > 0 and h == 0:
                            _need(v, am[("ctt", sg - 1, 15)])
                        ins = v.tensor_copy(out=cst[0:64, :], in_=psc[0:64, :])
                        e.v.inc(ins)
                    blk.vector(v_cst)
                    am[("cst", sg, h)] = (e.v, e.v.mark())

                    def t_rep(t, sg=sg, h=h):
                        _need(t, am[("rcp", sg, h)], m_bo)
                        if ("ctt", sg, h - 1) in am:
                            _need(t, am[("ctt", sg, h - 1)])
                        elif sg > 0 and h == 0:
                            _need(t, am[("ctt", sg - 1, 15)])
                        ins = t.matmul(ps_r[0:64, :], conso[64:65, :], rcp[64:65, :],
                                       start=True, stop=True)
                        e.t.inc(ins)
                    blk.tensor(t_rep)
                    am[("rep", sg, h)] = (e.t, e.t.mark())

                    if h % 2 == 0:
                        def v_ctt(v, sg=sg, h=h):
                            _need(v, am[("rep", sg, h)], am[("cst", sg, h)])
                            if sg > 0 and h == 0 and ("oev", sg - 1, 7) in om:
                                _need(v, om[("oev", sg - 1, 7)])
                            v.drain()
                            ins = v.tensor_tensor(out=ctxb[0:64, (h // 2) * 512:(h // 2 + 1) * 512],
                                                  in0=cst[0:64, :], in1=ps_r[0:64, :], op=ALU.mult)
                            e.v.inc(ins)
                        blk.vector(v_ctt)
                        am[("ctt", sg, h)] = (e.v, e.v.mark())
                    else:
                        def v_ctt(v, sg=sg, h=h):
                            _need(v, am[("rep", sg, h)], am[("cst", sg, h)])
                            if ("stgdma", sg, h - 2) in am:
                                _need(v, am[("stgdma", sg, h - 2)])
                            elif sg > 0 and h == 1:
                                _need(v, am[("stgdma", sg - 1, 15)])
                            v.drain()
                            ins = v.tensor_tensor(out=stg[:, :], in0=cst[0:64, :],
                                                  in1=ps_r[0:64, :], op=ALU.mult)
                            e.v.inc(ins)
                        blk.vector(v_ctt)
                        am[("ctt", sg, h)] = (e.v, e.v.mark())

                        def y_stg(y, sg=sg, h=h):
                            _need(y, am[("ctt", sg, h)])
                            if ("oev", sg - 1, 7) in om and h == 1:
                                _need(y, om[("oev", sg - 1, 7)])
                            ins = y.dma_start(out=ctxb[64:128, (h // 2) * 512:(h // 2 + 1) * 512],
                                              in_=stg[:, :])
                            e.y.inc(ins, 16)
                        blk.sync(y_stg)
                        am[("stgdma", sg, h)] = (e.y, e.y.mark())

                for tt_i in range(4):
                    tau = sg * 4 + tt_i
                    for half in range(2):
                        kk = tt_i * 2 + half
                        po = ps_s2[kk % 2][:, 0:512]
                        def t_o(t, sg=sg, tt_i=tt_i, half=half, kk=kk, po=po):
                            _need(t, m_wo, am[("exp", sg, 15, 7)])
                            for pr in range(8):
                                if tt_i == 0 and half == 0:
                                    _need(t, am[("ctt", sg, pr * 2)], am[("stgdma", sg, pr * 2 + 1)])
                            if ("oev", sg, kk - 2) in om:
                                _need(t, om[("oev", sg, kk - 2)])
                            elif sg > 0 and kk < 2:
                                _need(t, om[("oev", sg - 1, 6 + kk)])
                            for pr in range(8):
                                wo = e.wbig[pr // 4][:, (pr % 4) * 1024 + half * 512:(pr % 4) * 1024 + (half + 1) * 512]
                                ins = t.matmul(po,
                                               ctxb[:, pr * 512 + tt_i * 128: pr * 512 + (tt_i + 1) * 128],
                                               wo, start=(pr == 0), stop=(pr == 7))
                            e.t.inc(ins)
                        blk.tensor(t_o)
                        om[("omm", sg, kk)] = (e.t, e.t.mark())

                        def v_oadd(v, sg=sg, tau=tau, half=half, kk=kk, po=po):
                            _need(v, om[("omm", sg, kk)], m_bo_add)
                            dst = e.resid[:, tau * 1024 + half * 512: tau * 1024 + (half + 1) * 512]
                            ins = v.tensor_tensor(out=dst, in0=dst, in1=po, op=ALU.add)
                            e.v.inc(ins)
                        blk.vector(v_oadd)
                        om[("oev", sg, kk)] = (e.v, e.v.mark())

        e.m_block_end = [om[("oev", 3, 7)]]
        _emit_join(e)


def _emit_l1f(e, first):
    nc, blk = e.nc, e.blk
    NSL, TSL = 2, 1024
    NTS = TSL // 128
    with ExitStack() as ph:
        sb = lambda n, sh, dt: ph.enter_context(nc.sbuf_tensor(n, sh, dt))

        sm = _emit_ln_stats(e, {"resid": (e.v, e.v.mark())})

        def y_spill(y):
            _need(y, sm)
            for t in range(NT):
                ins = y.dma_start(out=e.spill[t * 128:(t + 1) * 128, :],
                                  in_=e.resid[:, t * 1024:(t + 1) * 1024])
                e.y.inc(ins, 16)
        blk.sync(y_spill)
        m_spill = (e.y, e.y.mark())

        e.resid_cm.__exit__(None, None, None)
        h1o_cm = nc.sbuf_tensor("l1f_h1o", [128, 32 * 1024], BF16, side="right")
        h1o = h1o_cm.__enter__()

        e.wsml = [sb(f"l1f_ws{i}", [128, 1024], BF16) for i in range(4)]
        e.wbig = [sb(f"l1f_wb{i}", [128, 4096], BF16) for i in range(2)]
        gbuf = sb("l1f_g", [128, 32 * TSL], BF16)
        t1 = sb("l1f_t1", [128, 512], BF16)
        t2 = [sb(f"l1f_t2{i}", [128, 512], BF16) for i in range(2)]
        scl = sb("l1f_scl", [128, 4 * 32], F32)
        ps_a = [ph.enter_context(nc.psum_tensor(f"l1f_pa{i}", [128, 512], F32)) for i in range(4)]
        ps_o = [ph.enter_context(nc.psum_tensor(f"l1f_po{i}", [128, 512], F32)) for i in range(2)]

        chs = _gch(e, "l1f_scl")
        def g_scales(g):
            for i, k in enumerate(("l1f_s1", "l1f_s2", "l1f_sh1", "l1f_sh2")):
                chs.inc(g.dma_start(out=scl[:, i * 32:(i + 1) * 32], in_=e.hbm[k]), 16)
        blk.gpsimd(g_scales)
        m_scl = (chs, chs.mark())

        wm, mm_m, ev_m, tt_m, out_m = {}, {}, {}, {}, {}

        def load_w12(k):
            o, sl = k % 32, k // 32
            ch = _gch(e, f"l1f_w12{k % 2}")
            def g_w(g, ch=ch):
                if ("w12u", k - 2) in mm_m:
                    _need(g, mm_m[("w12u", k - 2)])
                ins = g.dma_start(out=e.wsml[(k % 2) * 2][:, :],
                                  in_=e.hbm["l1f_w1"][o].rearrange("p c j -> p (c j)"))
                ch.inc(ins, 16)
                ins = g.dma_start(out=e.wsml[(k % 2) * 2 + 1][:, :],
                                  in_=e.hbm["l1f_w2"][o].rearrange("p c j -> p (c j)"))
                ch.inc(ins, 16)
            blk.gpsimd(g_w)
            wm[("w12", k)] = (ch, ch.mark())

        def load_h(which, k, extra=None):
            o = k % 32
            ch = _gch(e, f"l1f_h{k % 2}")
            def g_w(g, ch=ch):
                if (which + "u", k - 2) in mm_m:
                    _need(g, mm_m[(which + "u", k - 2)])
                if extra is not None:
                    _need(g, extra)
                ins = g.dma_start(out=e.wbig[k % 2][:, :],
                                  in_=e.hbm["l1f_" + which][o].rearrange("p c j -> p (c j)"))
                ch.inc(ins, 16)
            blk.gpsimd(g_w)
            wm[(which, k)] = (ch, ch.mark())

        load_w12(0)
        load_w12(1)

        for sl in range(NSL):
            with ExitStack() as slph:
                xnscope = ExitStack()
                xn = xnscope.enter_context(nc.sbuf_tensor(f"l1f_xn{sl}", [128, NC_ * TSL], BF16))
                lnm = _emit_ln_apply_from_spill(e, xnscope, xn, sl * NTS, NTS,
                                                f"l1f{sl}", sm, m_spill)

                for o in range(32):
                    k = sl * 32 + o
                    wg1, wg2 = e.wsml[(k % 2) * 2], e.wsml[(k % 2) * 2 + 1]
                    for sg in range(2):
                        j = k * 2 + sg
                        pga, pgb = ps_a[(j % 2) * 2], ps_a[(j % 2) * 2 + 1]
                        def t_mm(t, sl=sl, o=o, sg=sg, k=k, j=j, wg1=wg1, wg2=wg2, pga=pga, pgb=pgb):
                            _need(t, wm[("w12", k)], lnm["fm"])
                            if ("gdone", j - 2) in ev_m:
                                _need(t, ev_m[("gdone", j - 2)])
                            if ("gtt", j - 2) in tt_m:
                                _need(t, tt_m[("gtt", j - 2)])
                            if ("hdone", sl - 1, 1, 63) in ev_m and o == 0 and sg == 0:
                                _need(t, ev_m[("hdone", sl - 1, 1, 63)])
                            if sl > 0 and o == 0 and sg == 0:
                                _need(t, out_m[("ovlast", sl - 1)])
                            col0 = sg * 512
                            for c in range(NC_):
                                t.matmul(pga[:, :], wg1[:, c * 128:(c + 1) * 128],
                                         xn[:, c * TSL + col0: c * TSL + col0 + 512],
                                         start=(c == 0), stop=(c == 7))
                            for c in range(NC_):
                                ins = t.matmul(pgb[:, :], wg2[:, c * 128:(c + 1) * 128],
                                               xn[:, c * TSL + col0: c * TSL + col0 + 512],
                                               start=(c == 0), stop=(c == 7))
                            e.t.inc(ins)
                        blk.tensor(t_mm)
                        mm_m[("gmmj", j)] = (e.t, e.t.mark())
                        if sg == 1:
                            mm_m[("w12u", k)] = (e.t, e.t.mark())
                            if k + 2 < NSL * 32:
                                load_w12(k + 2)

                        def s_ev(s, sl=sl, o=o, sg=sg, j=j, pga=pga):
                            _need(s, mm_m[("gmmj", j)], m_scl)
                            if ("gtt", j - 1) in tt_m:
                                _need(s, tt_m[("gtt", j - 1)])
                            ins = s.activation(out=t1[:, :], in_=pga[:, :], func=AF.Silu,
                                               scale=scl[:, o:o + 1])
                            e.s.inc(ins)
                        blk.scalar(s_ev)
                        ev_m[("gdone", j)] = (e.s, e.s.mark())

                        def v_tt(v, sl=sl, o=o, sg=sg, j=j, pgb=pgb):
                            _need(v, ev_m[("gdone", j)], m_scl)
                            if ("gtt", j - 2) in tt_m:
                                _need(v, tt_m[("gtt", j - 2)])
                            if sl > 0 and o == 0 and sg == 0:
                                _need(v, out_m[("ovlast", sl - 1)])
                            v.tensor_scalar_mul(out=t2[j % 2][:, :], in0=pgb[:, :],
                                                scalar1=scl[:, 32 + o:32 + o + 1])
                            v.drain()
                            ins = v.tensor_tensor(out=gbuf[:, o * TSL + sg * 512: o * TSL + (sg + 1) * 512],
                                                  in0=t1[:, :], in1=t2[j % 2][:, :], op=ALU.mult)
                            e.v.inc(ins)
                        blk.vector(v_tt)
                        tt_m[("gtt", j)] = (e.v, e.v.mark())

                xnscope.close()

                for stage in range(2):
                    wkey = ("h1", "h2")[stage]
                    srcbuf = (gbuf, h1o)[stage]
                    dstbuf = (h1o, gbuf)[stage]
                    scol = (64, 96)[stage]
                    src_ready = tt_m[("gtt", sl * 64 + 63)] if stage == 0 else \
                        ev_m[("hdone", sl, 0, 63)]
                    for kk0 in range(2):
                        prev_user = None
                        if stage == 1:
                            prev_user = mm_m[("h1u", sl * 32 + 30 + kk0)]
                        elif sl > 0:
                            prev_user = mm_m[("h2u", (sl - 1) * 32 + 30 + kk0)]
                        load_h(wkey, sl * 32 + kk0, extra=prev_user)
                    for o in range(32):
                        k = sl * 32 + o
                        wgt = e.wbig[k % 2]
                        for sg in range(2):
                            j = o * 2 + sg
                            pp = ps_a[(j % 2) * 2 + stage]
                            def t_hmm(t, stage=stage, sl=sl, o=o, sg=sg, k=k, j=j,
                                      wgt=wgt, srcbuf=srcbuf, wkey=wkey, pp=pp,
                                      src_ready=src_ready):
                                _need(t, wm[(wkey, k)], src_ready)
                                if ("hdone", sl, stage, j - 2) in ev_m:
                                    _need(t, ev_m[("hdone", sl, stage, j - 2)])
                                if stage == 0 and o == 0 and sg == 0 and sl == 0:
                                    _need(t, m_spill)
                                col0 = sg * 512
                                for c in range(32):
                                    ins = t.matmul(pp[:, :], wgt[:, c * 128:(c + 1) * 128],
                                                   srcbuf[:, c * TSL + col0: c * TSL + col0 + 512],
                                                   start=(c == 0), stop=(c == 31))
                                e.t.inc(ins)
                            blk.tensor(t_hmm)
                            mm_m[("hmm", sl, stage, j)] = (e.t, e.t.mark())
                            if sg == 1:
                                mm_m[(wkey + "u", k)] = (e.t, e.t.mark())
                                if o + 2 < 32:
                                    load_h(wkey, k + 2)

                            def s_hev(s, stage=stage, sl=sl, o=o, sg=sg, j=j,
                                      dstbuf=dstbuf, scol=scol, pp=pp):
                                _need(s, mm_m[("hmm", sl, stage, j)])
                                ins = s.activation(out=dstbuf[:, o * TSL + sg * 512: o * TSL + (sg + 1) * 512],
                                                   in_=pp[:, :], func=AF.Gelu,
                                                   scale=scl[:, scol + o:scol + o + 1])
                                e.s.inc(ins)
                            blk.scalar(s_hev)
                            ev_m[("hdone", sl, stage, j)] = (e.s, e.s.mark())

                h2_ready = ev_m[("hdone", sl, 1, 63)]
                with ExitStack() as wph:
                    w3b = wph.enter_context(nc.sbuf_tensor(f"l1f_w3b{sl}", [128, 16 * 1024], BF16))
                    rq = wph.enter_context(nc.sbuf_tensor(f"l1f_rq{sl}", [128, 4 * 1024], F32))
                    s3r = wph.enter_context(nc.sbuf_tensor(f"l1f_s3r{sl}", [128, 1024], BF16))

                    ch3 = _gch(e, "l1f_s3")
                    def g_s3(g, ch3=ch3, s3r=s3r, sl=sl):
                        if sl > 0:
                            _need(g, out_m[("ovlast", sl - 1)])
                        ch3.inc(g.dma_start(out=s3r[:, :], in_=e.hbm["l1f_s3"]), 16)
                    blk.gpsimd(g_s3)
                    m_s3 = (ch3, ch3.mark())

                    def load_w3(hs, cpass, guard):
                        chw = _gch(e, "l1f_w3")
                        def g_w(g, chw=chw):
                            _need(g, guard)
                            for ci in range(16):
                                ins = g.dma_start(out=w3b[:, ci * 1024:(ci + 1) * 1024],
                                                  in_=e.hbm["l1f_w3"][cpass * 16 + ci])
                                chw.inc(ins, 16)
                        blk.gpsimd(g_w)
                        wm[("w3", sl, hs, cpass)] = (chw, chw.mark())

                    for hs in range(2):
                        def y_rq(y, sl=sl, hs=hs):
                            _need(y, m_spill)
                            if ("outdma", sl, hs - 1) in out_m:
                                _need(y, out_m[("outdma", sl, hs - 1)])
                            elif sl > 0 and hs == 0:
                                _need(y, out_m[("outdma", sl - 1, 1)])
                            r0 = sl * 1024 + hs * 512
                            ins = y.dma_start(
                                out=rq[:, :].rearrange("p (t f) -> p t f", f=1024),
                                in_=e.spill[r0:r0 + 512, :].rearrange("(t p) f -> p t f", p=128))
                            e.y.inc(ins, 16)
                        blk.sync(y_rq)
                        out_m[("rq", sl, hs)] = (e.y, e.y.mark())

                        for cpass in range(2):
                            key_prev = ("w3u", sl, hs, cpass - 1) if cpass else ("w3u", sl, hs - 1, 1)
                            if sl == 0 and hs == 0 and cpass == 0:
                                guard = (e.t, 0)
                            elif key_prev in wm:
                                guard = wm[key_prev]
                            else:
                                guard = wm[("w3u", sl - 1, 1, 1)]
                            load_w3(hs, cpass, guard)
                            for tt_i in range(4):
                                ti = hs * 4 + tt_i
                                for half in range(2):
                                    kk = tt_i * 2 + half
                                    po = ps_o[kk % 2]
                                    def t_w3(t, sl=sl, hs=hs, cpass=cpass, ti=ti, half=half, kk=kk, po=po):
                                        _need(t, wm[("w3", sl, hs, cpass)], h2_ready,
                                              out_m[("rq", sl, hs)])
                                        if ("ov", sl, hs, cpass, kk - 2) in out_m:
                                            _need(t, out_m[("ov", sl, hs, cpass, kk - 2)])
                                        elif cpass == 1 and ("ov", sl, hs, 0, 6 + kk) in out_m:
                                            _need(t, out_m[("ov", sl, hs, 0, 6 + kk)])
                                        elif hs == 1 and cpass == 0 and ("ov", sl, 0, 1, 6 + kk) in out_m:
                                            _need(t, out_m[("ov", sl, 0, 1, 6 + kk)])
                                        for ci in range(16):
                                            ins = t.matmul(po[:, :],
                                                           gbuf[:, (cpass * 16 + ci) * TSL + ti * 128:
                                                                (cpass * 16 + ci) * TSL + (ti + 1) * 128],
                                                           w3b[:, ci * 1024 + half * 512: ci * 1024 + (half + 1) * 512],
                                                           start=(ci == 0), stop=(ci == 15))
                                        e.t.inc(ins)
                                    blk.tensor(t_w3)
                                    out_m[("omm", sl, hs, cpass, kk)] = (e.t, e.t.mark())

                                    def v_out(v, sl=sl, hs=hs, cpass=cpass, tt_i=tt_i, half=half, kk=kk, po=po):
                                        _need(v, out_m[("omm", sl, hs, cpass, kk)], m_s3)
                                        dst = rq[:, tt_i * 1024 + half * 512: tt_i * 1024 + (half + 1) * 512]
                                        if ("ov", sl, hs, cpass, kk - 2) in out_m:
                                            _need(v, out_m[("ov", sl, hs, cpass, kk - 2)])
                                        v.tensor_tensor(out=t2[kk % 2][:, :], in0=po[:, :],
                                                        in1=s3r[:, half * 512:(half + 1) * 512], op=ALU.mult)
                                        v.drain()
                                        ins = v.tensor_tensor(out=dst, in0=dst, in1=t2[kk % 2][:, :], op=ALU.add)
                                        e.v.inc(ins)
                                    blk.vector(v_out)
                                    out_m[("ov", sl, hs, cpass, kk)] = (e.v, e.v.mark())
                            wm[("w3u", sl, hs, cpass)] = out_m[("omm", sl, hs, cpass, 7)]
                        out_m[("ovlast_h", sl, hs)] = out_m[("ov", sl, hs, 1, 7)]

                        cho2 = _gch(e, "l1f_out")
                        def y_out(y, sl=sl, hs=hs, cho2=cho2):
                            _need(y, out_m[("ovlast_h", sl, hs)])
                            r0 = sl * 1024 + hs * 512
                            ins = y.dma_start(
                                out=e.out[r0:r0 + 512, :].rearrange("(t p) f -> p t f", p=128),
                                in_=rq[:, :].rearrange("p (t f) -> p t f", f=1024))
                            cho2.inc(ins, 16)
                        blk.sync(y_out)
                        out_m[("outdma", sl, hs)] = (cho2, cho2.mark())
                    out_m[("ovlast", sl)] = out_m[("ovlast_h", sl, 1)]

        def y_fin(y):
            y.wait_ge(e.y.sem, e.y.mark())
            cho2 = _gch(e, "l1f_out")
            y.wait_ge(cho2.sem, cho2.mark())
        blk.sync(y_fin)
        _emit_join(e)



def kernel(**inputs):
    consts = _host_consts()
    e = build_nc()
    in_maps = []
    for core in range(8):
        d = _prep_core_inputs(core, inputs, consts)
        in_maps.append({k: d[k] for k in sorted(e.needed)})
    res = run_bass_kernel_spmd(e.nc, in_maps, core_ids=list(range(8)))
    return np.stack([r["out"] for r in res.results], axis=0)


if __name__ == "__main__":
    import reference as R
    inp = R.setup_inputs(0)
    inp = {k: np.asarray(v) for k, v in inp.items()}
    out = kernel(**inp)
    ref = np.asarray(R.reference(**inp))
    err = np.square(out - ref).mean() / np.square(ref).mean()
    print("resid-var ratio:", err)


# version 10 (final)
# speedup vs baseline: 1.0996x; 1.0996x over previous
import numpy as np
import ml_dtypes
import concourse.bass as bass
import concourse.mybir as mybir
from concourse.bass_utils import run_bass_kernel_spmd
from contextlib import ExitStack

F32 = mybir.dt.float32
BF16 = mybir.dt.bfloat16
I32 = mybir.dt.int32
AF = mybir.ActivationFunctionType
ALU = mybir.AluOpType
AX = mybir.AxisListType

S = 2048
DM = 1024
NT = S // 128
NC_ = DM // 128
BF = ml_dtypes.bfloat16
L0A_D_STRIDE0 = False


def _lhsT_tiles(wT, n_out_chunks=None):
    d_in, d_out = wT.shape
    ni, no = d_in // 128, d_out // 128
    a = wT.reshape(ni, 128, no, 128)
    return np.ascontiguousarray(a.transpose(2, 1, 0, 3))


def _rhs_tiles(wT):
    d_in, d_out = wT.shape
    return np.ascontiguousarray(wT.reshape(d_in // 128, 128, d_out))


def _chunk_cols(v):
    return np.ascontiguousarray(v.reshape(-1, 128).T)


def _rep(v, rows=128):
    return np.ascontiguousarray(np.broadcast_to(v[None, :], (rows, v.shape[0])))


def _host_consts():
    c = {}
    c["c_id"] = np.eye(128, dtype=BF)
    misc = np.zeros((128, 4), np.float32)
    misc[:, 0] = 1e-5
    c["c_misc"] = misc
    def l0mask(g0):
        m = np.zeros((128, 512), np.float32)
        for band in (0, 64):
            for tl in range(47):
                for o1 in range(16):
                    for sl in range(32):
                        o2 = sl + 15 - tl
                        ok = 0 <= o2 <= 15
                        if g0:
                            ok = ok and (tl >= 15) and (o1 <= sl)
                        if ok:
                            m[band + tl, o1 * 32 + sl] = 1.0
        return m.astype(BF)
    c["c_l0m_gen"] = l0mask(False)
    c["c_l0m_g0"] = l0mask(True)
    tri = np.zeros((128, 47), np.float32)
    for band in (0, 64):
        for k in range(47):
            tri[band + k, k:47] = 1.0
    c["c_l0tri"] = tri.astype(BF)
    def l0sel(g0):
        s = np.zeros((128, 512), np.float32)
        for band in (0, 64):
            for o1 in range(16):
                for sl in range(32):
                    col = o1 * 32 + sl
                    lo, hi = sl, sl + 15
                    if g0:
                        lo = max(lo, 15)
                        if o1 > sl:
                            lo, hi = 1, 0
                    s[band + 0, col] += -1.0
                    if lo <= hi:
                        s[band + lo, col] += 1.0
                        if hi + 1 <= 46:
                            s[band + hi + 1, col] += -1.0
        return (s * 1e30).astype(BF)
    c["c_l0sel_gen"] = l0sel(False)
    c["c_l0sel_g0"] = l0sel(True)
    pm = np.zeros((128, 512), np.float32)
    for band in (0, 64):
        for o1 in range(16):
            for sl in range(32):
                r = 15 + sl - o1
                if 0 <= r <= 46:
                    pm[band + r, o1 * 32 + sl] = 1.0
    c["c_l0pmat"] = pm.astype(BF)
    oab = np.zeros((128, 2), np.float32)
    oab[0:47, 0] = 1.0
    oab[64:111, 1] = 1.0
    c["c_l0ones"] = oab.astype(BF)
    rp = np.zeros((2, 128), np.float32)
    rp[0, 0:64] = 1.0
    rp[1, 64:128] = 1.0
    c["c_l0rep"] = rp
    on = np.zeros((128, 64), np.float32)
    on[64, :] = 1.0
    c["c_ones64"] = on
    m1 = np.zeros((128, 144), np.float32)
    for r in range(128):
        for cq in range(144):
            if 0 <= cq - r <= 16:
                m1[r, cq] = -1e30
    c["c_l1mask"] = m1.astype(BF)
    return c


def _prep_core_inputs(core, inp, consts):
    d = dict(consts)
    d["x"] = np.ascontiguousarray(inp["L0_attn_x"][core])
    ws = [np.ascontiguousarray(inp["L0_attn_W" + k].T) for k in ("q", "k", "kp", "v")]
    a = np.stack([w.reshape(8, 128, 8, 128) for w in ws], axis=0)
    d["l0a_wp"] = np.ascontiguousarray(a.transpose(3, 2, 0, 1, 4))
    d["l0a_wo"] = _rhs_tiles(np.ascontiguousarray(inp["L0_attn_Wo"].T))
    d["l0a_gamma"] = _rep(inp["L0_attn_gamma"])
    d["l0a_beta"] = _rep(inp["L0_attn_beta"])
    d["l0f_wg"] = _lhsT_tiles(np.ascontiguousarray(inp["L0_ffn_w_gate"].T))
    d["l0f_wu"] = _lhsT_tiles(np.ascontiguousarray(inp["L0_ffn_w_up"].T))
    d["l0f_wd"] = _rhs_tiles(np.ascontiguousarray(inp["L0_ffn_w_down"].T))
    wip = np.ascontiguousarray(inp["L1_attn_in_proj_weight"].T)
    d["l1a_wqk"] = _lhsT_tiles(np.ascontiguousarray(wip[:, 0:2048]))
    d["l1a_wv"] = _rhs_tiles(np.ascontiguousarray(wip[:, 2048:3072]))
    b = inp["L1_attn_in_proj_bias"]
    d["l1a_bqk"] = _chunk_cols(b[0:2048])
    bv = np.zeros((1040,), np.float32)
    for h in range(16):
        bv[h * 65:h * 65 + 64] = b[2048 + h * 64: 2048 + (h + 1) * 64]
    d["l1a_bv"] = _rep(bv)
    d["l1a_wo"] = _rhs_tiles(np.ascontiguousarray(inp["L1_attn_out_proj_weight"].T))
    d["l1a_bo"] = _rep(inp["L1_attn_out_proj_bias"])
    d["l1f_w1"] = _lhsT_tiles(np.ascontiguousarray(inp["L1_ffn_w1_w"].T))
    d["l1f_w2"] = _lhsT_tiles(np.ascontiguousarray(inp["L1_ffn_w2_w"].T))
    d["l1f_h1"] = _lhsT_tiles(np.ascontiguousarray(inp["L1_ffn_h1_w"].T))
    d["l1f_h2"] = _lhsT_tiles(np.ascontiguousarray(inp["L1_ffn_h2_w"].T))
    d["l1f_w3"] = _rhs_tiles(np.ascontiguousarray(inp["L1_ffn_w3_w"].T))
    d["l1f_s1"] = _chunk_cols(inp["L1_ffn_w1_s"])
    d["l1f_s2"] = _chunk_cols(inp["L1_ffn_w2_s"])
    d["l1f_sh1"] = _chunk_cols(inp["L1_ffn_h1_s"])
    d["l1f_sh2"] = _chunk_cols(inp["L1_ffn_h2_s"])
    d["l1f_s3"] = _rep(inp["L1_ffn_w3_s"])
    return d



class Ch:

    def __init__(self, nc, stack, name):
        self.sem = stack.enter_context(nc.semaphore(name))
        self.n = 0

    def inc(self, instr, k=1):
        instr.then_inc(self.sem, k)
        self.n += k
        return instr

    def mark(self):
        return self.n


class Env:
    pass


def build_nc(blocks=("l0a", "l0f", "l1a", "l1f"), dev_taps=()):
    nc = bass.Bass("TRN2", target_bir_lowering=False)
    e = Env()
    e.nc = nc
    e.blocks = blocks
    e.want_taps = set(dev_taps)

    dt_in = {
        "x": F32, "c_id": BF16, "c_misc": F32,
        "c_l0m_gen": BF16, "c_l0m_g0": BF16, "c_l0pmat": BF16, "c_l0ones": BF16,
        "c_l0tri": BF16, "c_l0sel_gen": BF16, "c_l0sel_g0": BF16,
        "c_l0rep": F32, "c_ones64": F32, "c_l1mask": BF16,
        "l0a_wp": F32, "l0a_wo": F32, "l0a_gamma": F32, "l0a_beta": F32,
        "l0f_wg": I32, "l0f_wu": I32, "l0f_wd": I32,
        "l1a_wqk": F32, "l1a_wv": F32, "l1a_bqk": F32, "l1a_bv": F32,
        "l1a_wo": F32, "l1a_bo": F32,
        "l1f_w1": F32, "l1f_w2": F32, "l1f_h1": F32, "l1f_h2": F32, "l1f_w3": F32,
        "l1f_s1": F32, "l1f_s2": F32, "l1f_sh1": F32, "l1f_sh2": F32, "l1f_s3": F32,
    }
    shapes = {
        "x": [S, DM], "c_id": [128, 128], "c_misc": [128, 4],
        "c_l0m_gen": [128, 512], "c_l0m_g0": [128, 512], "c_l0pmat": [128, 512],
        "c_l0tri": [128, 47], "c_l0sel_gen": [128, 512], "c_l0sel_g0": [128, 512],
        "c_l0ones": [128, 2], "c_l0rep": [2, 128], "c_ones64": [128, 64],
        "c_l1mask": [128, 144],
        "l0a_wp": [8, 128, 4, 8, 128], "l0a_wo": [8, 128, 1024],
        "l0a_gamma": [128, 1024], "l0a_beta": [128, 1024],
        "l0f_wg": [24, 128, 8, 128], "l0f_wu": [24, 128, 8, 128], "l0f_wd": [24, 128, 1024],
        "l1a_wqk": [16, 128, 8, 128], "l1a_wv": [8, 128, 1024],
        "l1a_bqk": [128, 16], "l1a_bv": [128, 1040],
        "l1a_wo": [8, 128, 1024], "l1a_bo": [128, 1024],
        "l1f_w1": [32, 128, 8, 128], "l1f_w2": [32, 128, 8, 128],
        "l1f_h1": [32, 128, 32, 128], "l1f_h2": [32, 128, 32, 128],
        "l1f_w3": [32, 128, 1024],
        "l1f_s1": [128, 32], "l1f_s2": [128, 32], "l1f_sh1": [128, 32],
        "l1f_sh2": [128, 32], "l1f_s3": [128, 1024],
    }
    e.hbm = {}
    needed = {"x", "c_id", "c_misc"}
    if "l0a" in blocks:
        needed |= {"l0a_wp", "l0a_wo", "l0a_gamma", "l0a_beta",
                   "c_l0tri", "c_l0sel_gen", "c_l0sel_g0", "c_l0ones", "c_l0rep"}
    if "l0f" in blocks:
        needed |= {"l0f_wg", "l0f_wu", "l0f_wd"}
    if "l1a" in blocks:
        needed |= {"l1a_wqk", "l1a_wv", "l1a_bqk", "l1a_bv", "l1a_wo", "l1a_bo",
                   "c_ones64", "c_l1mask"}
    if "l1f" in blocks:
        needed |= {"l1f_w1", "l1f_w2", "l1f_h1", "l1f_h2", "l1f_w3",
                   "l1f_s1", "l1f_s2", "l1f_sh1", "l1f_sh2", "l1f_s3"}
    e.needed = needed
    for k in sorted(needed):
        e.hbm[k] = nc.dram_tensor(k, shapes[k], dt_in[k], kind="ExternalInput").ap()
    e.out = nc.dram_tensor("out", [S, DM], F32, kind="ExternalOutput").ap()
    if "l1f" in blocks:
        e.spill = nc.dram_tensor("resid_spill", [S, DM], F32, kind="Internal").ap()

    with ExitStack() as stack:
        e.stack = stack
        sb = lambda name, shape, dt: stack.enter_context(nc.sbuf_tensor(name, shape, dt))
        e.resid_cm = nc.sbuf_tensor("resid", [128, NT * 1024], F32, side="right")
        e.resid = e.resid_cm.__enter__()
        e.c_id = sb("cid", [128, 128], BF16)
        e.c_misc = sb("cmisc", [128, 4], F32)
        e.mu = sb("mu", [128, NT], F32)
        e.rstd = sb("rstd", [128, NT], F32)
        e.lnb = sb("lnb", [128, NT], F32)
        e.bn = sb("bn", [128, NT * 12], F32)
        e.t = Ch(nc, stack, "t")
        e.s = Ch(nc, stack, "s")
        e.v = Ch(nc, stack, "v")
        e.y = Ch(nc, stack, "y")
        e.g = Ch(nc, stack, "g")
        e.bufs = {}
        e.gchan = {}

        with nc.body("main"):
            e.blk = _FlatBlk(nc)
            _emit_program(e)

    return e


class _FlatBlk:
    def __init__(self, nc):
        self.nc = nc

    def tensor(self, fn):
        fn(self.nc.tensor)

    def scalar(self, fn):
        fn(self.nc.scalar)

    def vector(self, fn):
        fn(self.nc.vector)

    def sync(self, fn):
        fn(self.nc.sync)

    def gpsimd(self, fn):
        fn(self.nc.gpsimd)


def _gch(e, name):
    if name not in e.gchan:
        e.gchan[name] = Ch(e.nc, e.stack, "gd_" + name)
    return e.gchan[name]


def _vfm_shift_view(vfm, g):
    import bass_rust as _br
    a = vfm[:, 0:512].copy()
    a.ap = _br.VecI64Pair([[2063, 128], [-1, 16], [1, 32]])
    a.offset = 15 + 32 * g
    return a


def _need(eng, *marks):
    cache = getattr(eng, "_wait_cache", None)
    if cache is None:
        cache = eng._wait_cache = {}
    for ch, m in marks:
        if m > 0 and cache.get(id(ch), -1) < m:
            eng.wait_ge(ch.sem, m)
            cache[id(ch)] = m


def _tap(e, name, ap, wait_marks):
    if name not in e.want_taps:
        return
    nc, blk = e.nc, e.blk
    shape = [ap.shape[0], int(np.prod(ap.shape[1:]))]
    t = nc.dram_tensor("tap_" + name, shape, ap.tensor.dtype, kind="ExternalOutput").ap()
    def y_t(y):
        _need(y, *wait_marks)
        e.y.inc(y.dma_start(out=t, in_=ap), 16)
    blk.sync(y_t)


def _emit_program(e):
    nc, blk = e.nc, e.blk

    def y_start(y):
        for t in range(NT):
            ins = y.dma_start(out=e.resid[:, t * 1024:(t + 1) * 1024],
                              in_=e.hbm["x"][t * 128:(t + 1) * 128, :])
            e.y.inc(ins, 16)
        e.y.inc(y.dma_start(out=e.c_id[:, :], in_=e.hbm["c_id"]), 16)
        e.y.inc(y.dma_start(out=e.c_misc[:, :], in_=e.hbm["c_misc"]), 16)
    blk.sync(y_start)
    e.m_x_loaded = (e.y, e.y.mark())

    first = True
    for b in e.blocks:
        if b == "l0a":
            _emit_l0a(e, first)
        elif b == "l0f":
            _emit_l0f(e, first)
        elif b == "l1a":
            _emit_l1a(e, first)
        elif b == "l1f":
            _emit_l1f(e, first)
        first = False

    if "l1f" not in e.blocks:
        mt, ms, mv = (e.t, e.t.mark()), (e.s, e.s.mark()), (e.v, e.v.mark())
        def y_out(y):
            _need(y, mt, ms, mv)
            for t in range(NT):
                ins = y.dma_start(out=e.out[t * 128:(t + 1) * 128, :],
                                  in_=e.resid[:, t * 1024:(t + 1) * 1024])
                e.y.inc(ins, 16)
            y.wait_ge(e.y.sem, e.y.mark())
        blk.sync(y_out)



def _emit_ln_stats(e, after_marks):
    nc, blk = e.nc, e.blk

    def v_stats(v):
        _need(v, after_marks.get("resid", (e.v, 0)), e.m_x_loaded)
        v.drain()
        for t in range(NT):
            v.bn_stats(e.bn[:, t * 12:t * 12 + 6], e.resid[:, t * 1024:t * 1024 + 512])
            v.bn_stats(e.bn[:, t * 12 + 6:t * 12 + 12], e.resid[:, t * 1024 + 512:t * 1024 + 1024])
        bnv = e.bn[:, :].rearrange("p (t g k) -> p t g k", g=4, k=3)
        msum, m2sum = e.mu[:, :], e.lnb[:, :]
        sqsum = e.rstd[:, :]
        v.drain()
        v.tensor_reduce(out=msum, in_=bnv[:, :, :, 1], axis=AX.X, op=ALU.add)
        v.tensor_reduce(out=m2sum, in_=bnv[:, :, :, 2], axis=AX.X, op=ALU.add)
        v.tensor_tensor(out=bnv[:, :, :, 0], in0=bnv[:, :, :, 1], in1=bnv[:, :, :, 1],
                        op=ALU.mult)
        v.drain()
        v.tensor_reduce(out=sqsum, in_=bnv[:, :, :, 0], axis=AX.X, op=ALU.add)
        v.tensor_scalar_mul(out=e.mu[:, :], in0=msum, scalar1=0.25)
        v.drain()
        v.tensor_scalar_mul(out=sqsum, in0=sqsum, scalar1=0.25)
        v.drain()
        v.scalar_tensor_tensor(out=sqsum, in0=m2sum, scalar=1.0 / 1024, in1=sqsum,
                               op0=ALU.mult, op1=ALU.add)
        v.drain()
        v.tensor_tensor(out=m2sum, in0=e.mu[:, :], in1=e.mu[:, :], op=ALU.mult)
        v.drain()
        ins = v.tensor_tensor(out=e.rstd[:, :], in0=sqsum, in1=m2sum, op=ALU.subtract)
        e.v.inc(ins)
    blk.vector(v_stats)
    mv = (e.v, e.v.mark())

    def s_rstd(s):
        _need(s, mv)
        s.activation(out=e.rstd[:, :], in_=e.rstd[:, :], func=AF.Ln,
                     bias=e.c_misc[:, 0:1], scale=1.0)
        s.drain()
        s.activation(out=e.rstd[:, :], in_=e.rstd[:, :], func=AF.Exp, scale=-0.5)
        ins = s.drain()
        e.s.inc(ins)
    blk.scalar(s_rstd)
    ms = (e.s, e.s.mark())

    def v_lnb(v):
        _need(v, ms)
        v.scalar_tensor_tensor(out=e.lnb[:, :], in0=e.mu[:, :], scalar=-1.0,
                               in1=e.rstd[:, :], op0=ALU.mult, op1=ALU.mult)
        e.v.inc(v.drain())
    blk.vector(v_lnb)
    return (e.v, e.v.mark())


def _emit_ln_apply(e, ph, xn_fm, pad, t0, nt, tag, stats_mark):
    nc, blk = e.nc, e.blk
    padf, padb = (pad, pad) if isinstance(pad, int) else pad
    W = padf + padb + nt * 128
    xn_tm = ph.enter_context(nc.sbuf_tensor("xn_tm_" + tag, [128, 1024], BF16))
    ps_tr = ph.enter_context(nc.psum_tensor("ps_tr_" + tag, [128, 512], F32))
    ps_tr2 = ph.enter_context(nc.psum_tensor("ps_tr2_" + tag, [128, 512], F32))

    if padf or padb:
        def v_pad(v):
            xv = xn_fm[:, :].rearrange("p (c w) -> p c w", w=W)
            if padf:
                v.memset(xv[:, :, 0:padf], 0)
            ins = v.memset(xv[:, :, padf + nt * 128:W], 0)
            e.v.inc(ins)
        blk.vector(v_pad)

    marks = {}
    for i in range(nt):
        t = t0 + i
        def s_apply(s, t=t, i=i):
            _need(s, stats_mark)
            if i >= 1:
                _need(s, marks[("tr", i - 1)])
            ins = s.activation(out=xn_tm[:, :], in_=e.resid[:, t * 1024:(t + 1) * 1024],
                               func=AF.Identity, bias=e.lnb[:, t:t + 1],
                               scale=e.rstd[:, t:t + 1])
            e.s.inc(ins)
        blk.scalar(s_apply)
        marks[("ap", i)] = (e.s, e.s.mark())

        def t_tr(tn, i=i):
            _need(tn, marks[("ap", i)])
            if i >= 1:
                _need(tn, marks[("ev", i - 1)])
            for c in range(NC_):
                ps = ps_tr if c < 4 else ps_tr2
                ins = tn.matmul(ps[:, (c % 4) * 128:(c % 4) * 128 + 128],
                                xn_tm[:, c * 128:(c + 1) * 128], e.c_id[:, :],
                                start=True, stop=True)
            e.t.inc(ins)
        blk.tensor(t_tr)
        marks[("tr", i)] = (e.t, e.t.mark())

        def s_ev(s, i=i):
            _need(s, marks[("tr", i)])
            xv = xn_fm[:, :].rearrange("p (c w) -> p c w", w=W)
            s.copy(out=xv[:, 0:4, padf + i * 128:padf + (i + 1) * 128],
                   in_=ps_tr[:, :].rearrange("p (c w) -> p c w", w=128))
            ins = s.copy(out=xv[:, 4:8, padf + i * 128:padf + (i + 1) * 128],
                         in_=ps_tr2[:, :].rearrange("p (c w) -> p c w", w=128))
            e.s.inc(ins)
        blk.scalar(s_ev)
        marks[("ev", i)] = (e.s, e.s.mark())

    return {"fm": (e.s, e.s.mark()), "stats": stats_mark}


def _emit_ln_apply_from_spill(e, ph, xn_fm, t0, nt, tag, stats_mark, m_spill):
    nc, blk = e.nc, e.blk
    W = nt * 128
    xn_tm = ph.enter_context(nc.sbuf_tensor("xn_tm_" + tag, [128, 1024], BF16))
    rstg = [ph.enter_context(nc.sbuf_tensor(f"rstg_{tag}{i}", [128, 1024], F32)) for i in range(2)]
    ps_tr = ph.enter_context(nc.psum_tensor("ps_tr_" + tag, [128, 512], F32))
    ps_tr2 = ph.enter_context(nc.psum_tensor("ps_tr2_" + tag, [128, 512], F32))
    chs = [_gch(e, f"lnspill_{tag}{i}") for i in range(2)]

    marks = {}
    for i in range(nt):
        t = t0 + i
        def y_ld(y, t=t, i=i):
            _need(y, m_spill)
            if i >= 2:
                _need(y, marks[("ap", i - 2)])
            ins = y.dma_start(out=rstg[i % 2][:, :], in_=e.spill[t * 128:(t + 1) * 128, :])
            chs[i % 2].inc(ins, 16)
        blk.sync(y_ld)
        marks[("ld", i)] = (chs[i % 2], chs[i % 2].mark())

        def s_apply(s, t=t, i=i):
            _need(s, stats_mark, marks[("ld", i)])
            if i >= 1:
                _need(s, marks[("tr", i - 1)])
            ins = s.activation(out=xn_tm[:, :], in_=rstg[i % 2][:, :],
                               func=AF.Identity, bias=e.lnb[:, t:t + 1],
                               scale=e.rstd[:, t:t + 1])
            e.s.inc(ins)
        blk.scalar(s_apply)
        marks[("ap", i)] = (e.s, e.s.mark())

        def t_tr(tn, i=i):
            _need(tn, marks[("ap", i)])
            if i >= 1:
                _need(tn, marks[("ev", i - 1)])
            for c in range(NC_):
                ps = ps_tr if c < 4 else ps_tr2
                ins = tn.matmul(ps[:, (c % 4) * 128:(c % 4) * 128 + 128],
                                xn_tm[:, c * 128:(c + 1) * 128], e.c_id[:, :],
                                start=True, stop=True)
            e.t.inc(ins)
        blk.tensor(t_tr)
        marks[("tr", i)] = (e.t, e.t.mark())

        def s_ev(s, i=i):
            _need(s, marks[("tr", i)])
            xv = xn_fm[:, :].rearrange("p (c w) -> p c w", w=W)
            s.copy(out=xv[:, 0:4, i * 128:(i + 1) * 128],
                   in_=ps_tr[:, :].rearrange("p (c w) -> p c w", w=128))
            ins = s.copy(out=xv[:, 4:8, i * 128:(i + 1) * 128],
                         in_=ps_tr2[:, :].rearrange("p (c w) -> p c w", w=128))
            e.s.inc(ins)
        blk.scalar(s_ev)
        marks[("ev", i)] = (e.s, e.s.mark())

    return {"fm": (e.s, e.s.mark()), "stats": stats_mark}


def _emit_ln(e, ph, xn_fm, pad, after_marks, tag="ln"):
    sm = _emit_ln_stats(e, after_marks)
    return _emit_ln_apply(e, ph, xn_fm, pad, 0, NT, tag, sm)



def _emit_l0f(e, first):
    nc, blk = e.nc, e.blk
    SC = 1.0 / 127.0
    NSL = 2
    TSL = 1024
    with ExitStack() as ph:
        sb = lambda n, sh, dt: ph.enter_context(nc.sbuf_tensor(n, sh, dt))
        e.wsml = [sb(f"l0f_ws{i}", [128, 1024], BF16) for i in range(4)]
        xn = sb("l0f_xn", [128, NC_ * 2048], BF16)
        hbuf = sb("l0f_h", [128, 24 * TSL], BF16)
        wdbuf = sb("l0f_wdb", [128, 12 * 1024], BF16)
        t1 = sb("l0f_t1", [128, 512], BF16)
        t2 = [sb(f"l0f_t2{i}", [128, 512], BF16) for i in range(2)]
        ps_gu = [ph.enter_context(nc.psum_tensor(f"l0f_pgu{i}", [128, 512], F32)) for i in range(4)]
        ps_d = [ph.enter_context(nc.psum_tensor(f"l0f_pd{i}", [128, 512], F32)) for i in range(2)]

        lnm = _emit_ln(e, ph, xn, 0, {"resid": (e.v, e.v.mark())}, tag="l0f")
        _tap(e, "xn", xn[:, :], [lnm["fm"]])
        _tap(e, "mu", e.mu[:, :], [lnm["stats"]])
        _tap(e, "rstd", e.rstd[:, :], [lnm["stats"]])

        wm, mm_m, ev_m, tt_m, dn_m = {}, {}, {}, {}, {}

        def load_gu(k):
            o = k % 24
            ch = _gch(e, f"l0f_gu{k % 2}")
            def g_w(g, ch=ch):
                if ("mm", k - 2) in mm_m:
                    _need(g, mm_m[("mm", k - 2)])
                ins = g.dma_start(out=e.wsml[(k % 2) * 2][:, :],
                                  in_=e.hbm["l0f_wg"][o].rearrange("p c j -> p (c j)"))
                ch.inc(ins, 16)
                ins = g.dma_start(out=e.wsml[(k % 2) * 2 + 1][:, :],
                                  in_=e.hbm["l0f_wu"][o].rearrange("p c j -> p (c j)"))
                ch.inc(ins, 16)
            blk.gpsimd(g_w)
            wm[("w", k)] = (ch, ch.mark())

        def load_wd(cpass, guard):
            ch = _gch(e, "l0f_wd")
            def g_w(g, ch=ch):
                _need(g, guard)
                for c in range(cpass * 12, cpass * 12 + 12):
                    ins = g.dma_start(out=wdbuf[:, (c % 12) * 1024:(c % 12 + 1) * 1024],
                                      in_=e.hbm["l0f_wd"][c])
                    ch.inc(ins, 16)
            blk.gpsimd(g_w)
            wm[("wd", cpass)] = (ch, ch.mark())

        load_gu(0)
        load_gu(1)
        load_wd(0, (e.t, 0))

        for sl in range(NSL):
            for o in range(24):
                k = sl * 24 + o
                wg, wu = e.wsml[(k % 2) * 2], e.wsml[(k % 2) * 2 + 1]
                for sg in range(2):
                    j = k * 2 + sg
                    pg, pu = ps_gu[(j % 2) * 2], ps_gu[(j % 2) * 2 + 1]
                    def t_mm(t, sl=sl, o=o, sg=sg, k=k, j=j, wg=wg, wu=wu, pg=pg, pu=pu):
                        _need(t, wm[("w", k)])
                        if j == 0:
                            _need(t, lnm["fm"])
                        if ("done", j - 2) in ev_m:
                            _need(t, ev_m[("done", j - 2)])
                        if ("tt", j - 2) in tt_m:
                            _need(t, tt_m[("tt", j - 2)])
                        if sl > 0 and o == 0 and sg == 0:
                            _need(t, dn_m[("add", sl - 1, 31)])
                        col0 = sl * 1024 + sg * 512
                        for c in range(NC_):
                            t.matmul(pg[:, :], wg[:, c * 128:(c + 1) * 128],
                                     xn[:, c * 2048 + col0: c * 2048 + col0 + 512],
                                     start=(c == 0), stop=(c == 7))
                        for c in range(NC_):
                            ins = t.matmul(pu[:, :], wu[:, c * 128:(c + 1) * 128],
                                           xn[:, c * 2048 + col0: c * 2048 + col0 + 512],
                                           start=(c == 0), stop=(c == 7))
                        e.t.inc(ins)
                    blk.tensor(t_mm)
                    mm_m[("mmj", j)] = (e.t, e.t.mark())
                    if sg == 1:
                        mm_m[("mm", k)] = (e.t, e.t.mark())
                        if k + 2 < NSL * 24:
                            load_gu(k + 2)

                    def s_ev(s, j=j, pg=pg):
                        _need(s, mm_m[("mmj", j)])
                        if ("tt", j - 1) in tt_m:
                            _need(s, tt_m[("tt", j - 1)])
                        ins = s.activation(out=t1[:, :], in_=pg[:, :], func=AF.Silu, scale=SC)
                        e.s.inc(ins)
                    blk.scalar(s_ev)
                    ev_m[("done", j)] = (e.s, e.s.mark())

                    def v_tt(v, sl=sl, o=o, sg=sg, j=j, pu=pu):
                        _need(v, ev_m[("done", j)])
                        if ("tt", j - 2) in tt_m:
                            _need(v, tt_m[("tt", j - 2)])
                        if sl > 0 and o == 0 and sg == 0:
                            _need(v, dn_m[("add", sl - 1, 31)])
                        v.tensor_scalar_mul(out=t2[j % 2][:, :], in0=pu[:, :], scalar1=SC)
                        v.drain()
                        ins = v.tensor_tensor(out=hbuf[:, o * TSL + sg * 512: o * TSL + (sg + 1) * 512],
                                              in0=t1[:, :], in1=t2[j % 2][:, :], op=ALU.mult)
                        e.v.inc(ins)
                    blk.vector(v_tt)
                    tt_m[("tt", j)] = (e.v, e.v.mark())

            h_ready = tt_m[("tt", sl * 48 + 47)]
            if sl == 0:
                _tap(e, "h0", hbuf[:, :], [h_ready])
            for cpass in range(2):
                for ti in range(8):
                    for half in range(2):
                        k2 = cpass * 16 + ti * 2 + half
                        pd = ps_d[k2 % 2]
                        def t_dn(t, sl=sl, cpass=cpass, ti=ti, half=half, k2=k2, pd=pd):
                            _need(t, wm[("wd", cpass)], h_ready)
                            if ("add", sl, k2 - 2) in dn_m:
                                _need(t, dn_m[("add", sl, k2 - 2)])
                            elif sl > 0 and k2 < 2:
                                _need(t, dn_m[("add", sl - 1, 30 + k2)])
                            for ci in range(12):
                                c = cpass * 12 + ci
                                ins = t.matmul(pd[:, :],
                                               hbuf[:, c * TSL + ti * 128: c * TSL + (ti + 1) * 128],
                                               wdbuf[:, ci * 1024 + half * 512: ci * 1024 + (half + 1) * 512],
                                               start=(ci == 0), stop=(ci == 11))
                            e.t.inc(ins)
                        blk.tensor(t_dn)
                        dn_m[("mm", sl, k2)] = (e.t, e.t.mark())

                        def v_add(v, sl=sl, ti=ti, half=half, k2=k2, pd=pd):
                            _need(v, dn_m[("mm", sl, k2)])
                            t_g = sl * 8 + ti
                            dst = e.resid[:, t_g * 1024 + half * 512: t_g * 1024 + (half + 1) * 512]
                            ins = v.scalar_tensor_tensor(out=dst, in0=pd[:, :], scalar=SC,
                                                         in1=dst, op0=ALU.mult, op1=ALU.add)
                            e.v.inc(ins)
                        blk.vector(v_add)
                        dn_m[("add", sl, k2)] = (e.v, e.v.mark())
                if cpass == 0:
                    load_wd(1, dn_m[("mm", sl, 15)])
                elif sl + 1 < NSL:
                    load_wd(0, dn_m[("mm", sl, 31)])

        e.m_block_end = [dn_m[("add", NSL - 1, 31)]]
        _emit_join(e)


def _emit_join(e):
    blk = e.blk
    mt, ms, mv = (e.t, e.t.mark()), (e.s, e.s.mark()), (e.v, e.v.mark())
    mg, my = (e.g, e.g.mark()), (e.y, e.y.mark())
    def t_j(t): _need(t, ms, mv)
    def s_j(s):
        s.drain()
        _need(s, mt, mv)
    def v_j(v):
        v.drain()
        _need(v, mt, ms)
    def g_j(g): _need(g, mt, ms, mv)
    def y_j(y): _need(y, mt, ms, mv)
    blk.tensor(t_j); blk.scalar(s_j); blk.vector(v_j); blk.gpsimd(g_j); blk.sync(y_j)


def _emit_l0a(e, first):
    nc, blk = e.nc, e.blk
    PF, PB = 16, 128
    WCH = PF + 2048 + PB
    ROT_A = {0: 15, 1: 111, 2: 79, 3: 47}
    ROT_B = {0: 79, 1: 47, 2: 15, 3: 111}
    with ExitStack() as ph:
        sb = lambda n, sh, dt: ph.enter_context(nc.sbuf_tensor(n, sh, dt))
        e.wbig = [sb(f"l0a_wb{i}", [128, 4096], BF16) for i in range(2)]
        xnp = sb("l0a_xn", [128, NC_ * WCH], BF16)
        zall = sb("l0a_z", [128, 8 * 2048], BF16)
        trib = sb("l0a_tri", [128, 47], BF16)
        selg = sb("l0a_selg", [128, 512], BF16)
        sel0 = sb("l0a_sel0", [128, 512], BF16)
        oab = sb("l0a_oab", [128, 2], BF16)
        rep2 = sb("l0a_rp2", [2, 128], F32)
        inner = ExitStack()
        sbi = lambda n, sh, dt: inner.enter_context(nc.sbuf_tensor(n, sh, dt))
        qb = sbi("l0a_qb", [128, 2048], BF16)
        kb = sbi("l0a_kb", [128, 2063], BF16)
        kpb = sbi("l0a_kpb", [128, 2063], BF16)
        vfm = sbi("l0a_vfm", [128, 2063], BF16)
        rots = {}
        for hd in ("a", "b"):
            for r in (15, 47, 79, 111):
                rots[(hd, r)] = sbi(f"l0a_r{hd}{r}", [128, 17 * 64], BF16)
        ubuf = sbi("l0a_u", [128, 16 * 256], BF16)
        expb = [sbi(f"l0a_e{i}", [128, 512], BF16) for i in range(2)]
        wbf = [sbi(f"l0a_w{i}", [128, 512], BF16) for i in range(2)]
        zsl = sbi("l0a_zs", [128, 512], F32)
        dsl = sbi("l0a_ds", [2, 512], F32)

        sm = _emit_ln_stats(e, {"resid": (e.v, e.v.mark())})
        lnscope = ExitStack()
        lnm = _emit_ln_apply(e, lnscope, xnp, (PF, PB), 0, NT, "l0a", sm)
        lnscope.close()

        chk = _gch(e, "l0a_con")
        def g_consts(g):
            chk.inc(g.dma_start(out=trib[:, :], in_=e.hbm["c_l0tri"]), 16)
            chk.inc(g.dma_start(out=selg[:, :], in_=e.hbm["c_l0sel_gen"]), 16)
            chk.inc(g.dma_start(out=sel0[:, :], in_=e.hbm["c_l0sel_g0"]), 16)
            chk.inc(g.dma_start(out=oab[:, :], in_=e.hbm["c_l0ones"]), 16)
            chk.inc(g.dma_start(out=rep2[:, :], in_=e.hbm["c_l0rep"]), 16)
        blk.gpsimd(g_consts)
        m_con = (chk, chk.mark())

        wm, km, gm, dv = {}, {}, {}, {}

        def load_wp(pr):
            ch = _gch(e, f"l0a_wp{pr % 2}")
            def g_w(g, ch=ch):
                if ("wpu", pr - 2) in km:
                    _need(g, km[("wpu", pr - 2)])
                ins = g.dma_start(out=e.wbig[pr % 2][:, :],
                                  in_=e.hbm["l0a_wp"][pr].rearrange("p w c j -> p (w c j)"))
                ch.inc(ins, 16)
            blk.gpsimd(g_w)
            wm[("wp", pr)] = (ch, ch.mark())

        load_wp(0)
        load_wp(1)

        ps_a = [inner.enter_context(nc.psum_tensor(f"l0a_pa{i}", [128, 512], F32)) for i in range(3)]
        ps_pj = [ps_a[0], ps_a[1]]
        ps_r = [inner.enter_context(nc.psum_tensor(f"l0a_pr{i}", [128, 512], F32)) for i in range(2)]
        ps_d2 = [inner.enter_context(nc.psum_tensor(f"l0a_pd{i}", [128, 512], F32)) for i in range(2)]
        ps_rep = inner.enter_context(nc.psum_tensor("l0a_prp", [128, 512], F32))
        def v_zero(v):
            _need(v, lnm["fm"])
            for pa in ps_a:
                v.memset(pa[32:64, :], 0)
                v.memset(pa[96:128, :], 0)
            e.v.inc(v.drain())
        blk.vector(v_zero)
        m_zero = (e.v, e.v.mark())

        for pr in range(8):
            wp = e.wbig[pr % 2]
            for wi, dst, off in ((0, qb, 0), (1, kb, 15), (2, kpb, 15), (3, vfm, 15)):
                for sg in range(4):
                    pp = ps_pj[sg % 2]
                    def t_p(t, pr=pr, wi=wi, sg=sg, pp=pp):
                        _need(t, wm[("wp", pr)], lnm["fm"])
                        if pr == 0 and wi == 0 and sg == 0:
                            _need(t, m_zero)
                        if pr == 1 and getattr(e, "tap_guard", None):
                            _need(t, e.tap_guard)
                        key = ("pev", pr, wi, sg - 2)
                        key2 = ("pev", pr, wi - 1, 2 + sg)
                        key3 = ("pev", pr - 1, 3, 2 + sg)
                        for kk in (key, key2, key3):
                            if kk in km:
                                _need(t, km[kk])
                                break
                        if wi == 0 and sg < 2 and pr > 0:
                            _need(t, gm[("exp", pr - 1, 63)])
                        for c in range(NC_):
                            ins = t.matmul(pp[:, :], wp[:, wi * 1024 + c * 128: wi * 1024 + (c + 1) * 128],
                                           xnp[:, c * WCH + PF + sg * 512: c * WCH + PF + (sg + 1) * 512],
                                           start=(c == 0), stop=(c == 7))
                        e.t.inc(ins)
                    blk.tensor(t_p)
                    km[("pmm", pr, wi, sg)] = (e.t, e.t.mark())

                    def s_p(s, pr=pr, wi=wi, sg=sg, dst=dst, off=off, pp=pp):
                        _need(s, km[("pmm", pr, wi, sg)])
                        ins = s.activation(out=dst[:, off + sg * 512: off + (sg + 1) * 512],
                                           in_=pp[:, :], func=AF.Identity,
                                           scale=0.125 if wi == 0 else 1.0)
                        e.s.inc(ins)
                    blk.scalar(s_p)
                    km[("pev", pr, wi, sg)] = (e.s, e.s.mark())

            def v_kp0(v, pr=pr):
                if ("gdone", pr - 1, 63) in gm:
                    _need(v, gm[("gdone", pr - 1, 63)])
                v.memset(kb[:, 0:15], 0)
                v.memset(kpb[:, 0:15], 0)
                ins = v.memset(vfm[:, 0:15], 0)
                e.v.inc(ins)
            blk.vector(v_kp0)
            km[("kp0", pr)] = (e.v, e.v.mark())

            for tau in range(17):
                pp = ps_pj[tau % 2]
                def t_v(t, pr=pr, tau=tau, pp=pp):
                    _need(t, wm[("wp", pr)], lnm["fm"])
                    key = ("vev", pr, tau - 2)
                    if key in km:
                        _need(t, km[key])
                    elif tau < 2:
                        _need(t, km[("pev", pr, 3, 2 + tau)])
                    for c in range(NC_):
                        ins = t.matmul(pp[:, 0:128],
                                       xnp[:, c * WCH + PF + 128 * tau - 15: c * WCH + PF + 128 * tau + 113],
                                       wp[:, 3 * 1024 + c * 128: 3 * 1024 + (c + 1) * 128],
                                       start=(c == 0), stop=(c == 7))
                    e.t.inc(ins)
                blk.tensor(t_v)
                km[("vmm", pr, tau)] = (e.t, e.t.mark())

                def s_v(s, pr=pr, tau=tau, pp=pp):
                    _need(s, km[("vmm", pr, tau)])
                    if ("rotdma", pr - 1) in km and tau == 0:
                        _need(s, km[("rotdma", pr - 1)])
                    s.copy(out=rots[("a", 15)][:, tau * 64:(tau + 1) * 64], in_=pp[:, 0:64])
                    ins = s.copy(out=rots[("b", 15)][:, tau * 64:(tau + 1) * 64], in_=pp[:, 64:128])
                    e.s.inc(ins)
                blk.scalar(s_v)
                km[("vev", pr, tau)] = (e.s, e.s.mark())
            m_v15 = (e.s, e.s.mark())

            def y_rot(y, pr=pr):
                _need(y, m_v15)
                if ("gdone", pr - 1, 63) in gm:
                    _need(y, gm[("gdone", pr - 1, 63)])
                for hd in ("a", "b"):
                    sa = rots[(hd, 15)]
                    for kk_, rdst in ((1, 47), (2, 79), (3, 111)):
                        da = rots[(hd, rdst)]
                        sh = 32 * kk_
                        ins = y.dma_start(out=da[sh:128, :], in_=sa[0:128 - sh, :])
                        e.y.inc(ins, 16)
                        ins = y.dma_start(out=da[0:sh, 64:17 * 64], in_=sa[128 - sh:128, 0:16 * 64])
                        e.y.inc(ins, 16)
            blk.sync(y_rot)
            km[("rotdma0", pr)] = (e.y, e.y.mark())

            def v_rot0(v, pr=pr):
                _need(v, km[("rotdma0", pr)])
                for hd in ("a", "b"):
                    for kk_, rdst in ((1, 47), (2, 79), (3, 111)):
                        v.memset(rots[(hd, rdst)][0:32 * kk_, 0:64], 0)
                e.v.inc(v.drain())
            blk.vector(v_rot0)
            km[("rotdma", pr)] = (e.v, e.v.mark())

            def emit_u(pr, sgq, after_g):
                def v_u(v, pr=pr, sgq=sgq, after_g=after_g):
                    _need(v, km[("pev", pr, 0, min(3, (sgq * 256 + 255) // 512))],
                          km[("pev", pr, 1, min(3, (sgq * 256 + 255) // 512))],
                          km[("kp0", pr)])
                    if ("amm", pr, after_g) in gm:
                        _need(v, gm[("amm", pr, after_g)])
                    elif pr > 0 and sgq == 0:
                        _need(v, gm[("amm", pr - 1, 63)])
                    for o1 in range(16):
                        ins = v.tensor_tensor(
                            out=ubuf[:, o1 * 256:(o1 + 1) * 256],
                            in0=qb[:, sgq * 256:(sgq + 1) * 256],
                            in1=kb[:, 15 - o1 + sgq * 256: 15 - o1 + (sgq + 1) * 256],
                            op=ALU.mult)
                    e.v.inc(ins)
                blk.vector(v_u)
                gm[("u", pr, sgq)] = (e.v, e.v.mark())

            def emit_a(pr, g):
                if g % 8 == 0:
                    emit_u(pr, g // 8, g - 1)
                pa = ps_a[g % 3]
                def t_a(t, pr=pr, g=g, pa=pa):
                    _need(t, gm[("u", pr, g // 8)], m_zero, m_con)
                    if g < 2:
                        _need(t, km[("vev", pr, 16)])
                    if ("exp", pr, g - 3) in gm:
                        _need(t, gm[("exp", pr, g - 3)])
                    elif ("exp", pr - 1, 61 + g) in gm:
                        _need(t, gm[("exp", pr - 1, 61 + g)])
                    soff = (g % 8) * 32
                    rhs_a = ubuf[0:64, :].rearrange("p (o s) -> p o s", o=16)[:, :, soff:soff + 32]
                    rhs_b = ubuf[64:128, :].rearrange("p (o s) -> p o s", o=16)[:, :, soff:soff + 32]
                    sel = sel0 if g == 0 else selg
                    t.matmul(pa[0:47, :], kpb[0:64, 32 * g:32 * g + 47], rhs_a,
                             start=True, stop=False)
                    t.matmul(pa[0:47, :], trib[0:47, :], sel[0:47, :],
                             start=False, stop=True)
                    t.matmul(pa[64:111, :], kpb[64:128, 32 * g:32 * g + 47], rhs_b,
                             start=True, stop=False, tile_position=(64, 64))
                    ins = t.matmul(pa[64:111, :], trib[64:111, :], sel[64:111, :],
                                   start=False, stop=True, tile_position=(64, 64))
                    e.t.inc(ins)
                blk.tensor(t_a)
                gm[("amm", pr, g)] = (e.t, e.t.mark())

            def emit_exp_mask(pr, g):
                pa = ps_a[g % 3]
                def s_e(s, pr=pr, g=g, pa=pa):
                    _need(s, gm[("amm", pr, g)])
                    if ("rmm", pr, g - 2) in gm:
                        _need(s, gm[("rmm", pr, g - 2)])
                    elif ("rmm", pr - 1, 62 + g) in gm:
                        _need(s, gm[("rmm", pr - 1, 62 + g)])
                    ins = s.activation(out=expb[g % 2][:, :], in_=pa[:, :], func=AF.Exp)
                    e.s.inc(ins)
                blk.scalar(s_e)
                gm[("exp", pr, g)] = (e.s, e.s.mark())

            def emit_rgrp(pr, g):
                psr = ps_r[g % 2]
                def t_r(t, pr=pr, g=g, psr=psr):
                    _need(t, gm[("exp", pr, g)], km[("rotdma", pr)])
                    if ("wtt", pr, g - 2) in gm:
                        _need(t, gm[("wtt", pr, g - 2)])
                    elif ("wtt", pr - 1, 62 + g) in gm:
                        _need(t, gm[("wtt", pr - 1, 62 + g)])
                    if ("dred", pr, g - 2) in gm:
                        _need(t, gm[("dred", pr, g - 2)])
                    elif ("dred", pr - 1, 62 + g) in gm:
                        _need(t, gm[("dred", pr - 1, 62 + g)])
                    me = expb[g % 2]
                    w = 32 * g - 15
                    ra = rots[("a", ROT_A[g % 4])]
                    rb = rots[("b", ROT_B[g % 4])]
                    ta = (w - 0 + ROT_A[g % 4]) // 128
                    tb = (w - 64 + ROT_B[g % 4]) // 128
                    t.matmul(psr[0:64, :], ra[0:47, ta * 64:(ta + 1) * 64], me[0:47, :],
                             start=True, stop=True)
                    t.matmul(psr[64:128, :], rb[64:111, tb * 64:(tb + 1) * 64], me[64:111, :],
                             start=True, stop=True, tile_position=(64, 64))
                    ins = t.matmul(ps_d2[g % 2][0:2, :], oab[:, :], me[:, :],
                                   start=True, stop=True)
                    e.t.inc(ins)
                blk.tensor(t_r)
                gm[("rmm", pr, g)] = (e.t, e.t.mark())

                def v_wtt(v, pr=pr, g=g, psr=psr):
                    _need(v, gm[("rmm", pr, g)])
                    if ("wred", pr, g - 2) in gm:
                        _need(v, gm[("wred", pr, g - 2)])
                    elif ("wred", pr - 1, 62 + g) in gm:
                        _need(v, gm[("wred", pr - 1, 62 + g)])
                    ins = v.tensor_tensor(out=wbf[g % 2][:, :].rearrange("p (o s) -> p o s", o=16),
                                    in0=psr[:, :].rearrange("p (o s) -> p o s", o=16),
                                    in1=_vfm_shift_view(vfm, g), op=ALU.mult)
                    e.v.inc(ins)
                blk.vector(v_wtt)
                gm[("wtt", pr, g)] = (e.v, e.v.mark())

                def v_dred(v, pr=pr, g=g):
                    _need(v, gm[("rmm", pr, g)])
                    if g % 16 == 0:
                        if ("dinv", pr, g // 16 - 1) in dv:
                            _need(v, dv[("dinv", pr, g // 16 - 1)])
                        elif pr > 0 and ("dinv", pr - 1, 3) in dv:
                            _need(v, dv[("dinv", pr - 1, 3)])
                    ins = v.tensor_reduce(out=dsl[0:2, (g % 16) * 32:(g % 16 + 1) * 32],
                                    in_=ps_d2[g % 2][0:2, :].rearrange("p (o s) -> p s o", o=16),
                                    axis=AX.X, op=ALU.add)
                    e.v.inc(ins)
                blk.vector(v_dred)
                gm[("dred", pr, g)] = (e.v, e.v.mark())
                gm[("gdone", pr, g)] = gm[("dred", pr, g)]

                def v_wred(v, pr=pr, g=g):
                    _need(v, gm[("wtt", pr, g)])
                    if g % 16 == 0:
                        if ("ztt", pr, g // 16 - 1) in dv:
                            _need(v, dv[("ztt", pr, g // 16 - 1)])
                        elif g == 0 and pr > 0:
                            _need(v, dv[("ztt", pr - 1, 3)])
                    ins = v.tensor_reduce(out=zsl[:, (g % 16) * 32:(g % 16 + 1) * 32],
                                    in_=wbf[g % 2][:, :].rearrange("p (o s) -> p s o", o=16),
                                    axis=AX.X, op=ALU.add)
                    e.v.inc(ins)
                blk.vector(v_wred)
                gm[("wred", pr, g)] = (e.v, e.v.mark())

                if g % 16 == 15:
                    sgz = g // 16
                    def v_div(v, pr=pr, sgz=sgz):
                        _need(v, gm[("wred", pr, sgz * 16 + 15)], gm[("dred", pr, sgz * 16 + 15)])
                        if ("rep", pr, sgz - 1) in dv:
                            _need(v, dv[("rep", pr, sgz - 1)])
                        elif pr > 0 and sgz == 0:
                            _need(v, dv[("rep", pr - 1, 3)])
                        v.drain()
                        ins = v.reciprocal(out=dsl[0:2, :], in_=dsl[0:2, :])
                        e.v.inc(ins)
                    blk.vector(v_div)
                    dv[("dinv", pr, sgz)] = (e.v, e.v.mark())

                    def t_rep(t, pr=pr, sgz=sgz):
                        _need(t, dv[("dinv", pr, sgz)], m_con)
                        if ("ztt", pr, sgz - 1) in dv:
                            _need(t, dv[("ztt", pr, sgz - 1)])
                        elif pr > 0 and sgz == 0:
                            _need(t, dv[("ztt", pr - 1, 3)])
                        ins = t.matmul(ps_rep[:, :], rep2[:, :], dsl[0:2, :],
                                       start=True, stop=True)
                        e.t.inc(ins)
                    blk.tensor(t_rep)
                    dv[("rep", pr, sgz)] = (e.t, e.t.mark())

                    def v_z(v, pr=pr, sgz=sgz):
                        _need(v, dv[("rep", pr, sgz)])
                        ins = v.tensor_tensor(out=zall[:, pr * 2048 + sgz * 512: pr * 2048 + (sgz + 1) * 512],
                                              in0=zsl[:, :], in1=ps_rep[:, :], op=ALU.mult)
                        e.v.inc(ins)
                    blk.vector(v_z)
                    dv[("ztt", pr, sgz)] = (e.v, e.v.mark())

            emit_a(pr, 0)
            emit_a(pr, 1)
            for g in range(64):
                emit_exp_mask(pr, g)
                emit_rgrp(pr, g)
                if g + 2 < 64:
                    emit_a(pr, g + 2)

            km[("wpu", pr)] = km[("vmm", pr, 16)]
            if pr + 2 < 8:
                load_wp(pr + 2)
            if pr == 0 and "q0" in e.want_taps:
                _tap(e, "q0", qb[:, :], [gm[("gdone", 0, 63)]])
                _tap(e, "kp0", kpb[:, :], [gm[("gdone", 0, 63)]])
                _tap(e, "k0", kb[:, :], [gm[("gdone", 0, 63)]])
                _tap(e, "r15a", rots[("a", 15)][:, :], [gm[("gdone", 0, 63)]])
                e.tap_guard = (e.y, e.y.mark())

        m_z_all = (e.v, e.v.mark())
        _tap(e, "zall", zall[:, :], [m_z_all])
        _tap(e, "q7", qb[:, :], [m_z_all])
        _tap(e, "kp7", kpb[:, :], [m_z_all])
        inner.close()

        chwo = _gch(e, "l0a_wo")
        def g_wo(g):
            _need(g, km[("wpu", 7)], km[("vmm", 7, 16)])
            for c in range(8):
                ins = g.dma_start(out=e.wbig[c // 4][:, (c % 4) * 1024:(c % 4 + 1) * 1024],
                                  in_=e.hbm["l0a_wo"][c])
                chwo.inc(ins, 16)
        blk.gpsimd(g_wo)
        m_wo = (chwo, chwo.mark())

        with ExitStack() as eph:
            gam = eph.enter_context(nc.sbuf_tensor("l0a_gam", [128, 1024], F32))
            bet = eph.enter_context(nc.sbuf_tensor("l0a_bet", [128, 1024], F32))
            xtm = eph.enter_context(nc.sbuf_tensor("l0a_xtm", [128, 1024], BF16))
            hb = eph.enter_context(nc.sbuf_tensor("l0a_hb", [128, 1024], F32))
            tb = eph.enter_context(nc.sbuf_tensor("l0a_tb", [128, 1024], F32))
            bn2 = eph.enter_context(nc.sbuf_tensor("l0a_bn2", [128, 12], F32))
            st2 = eph.enter_context(nc.sbuf_tensor("l0a_st2", [128, 4], F32))
            ps_o = [eph.enter_context(nc.psum_tensor(f"l0a_po{i}", [128, 512], F32)) for i in range(2)]

            chgb = _gch(e, "l0a_gb")
            def g_gb(g):
                _need(g, m_z_all)
                chgb.inc(g.dma_start(out=gam[:, :], in_=e.hbm["l0a_gamma"]), 16)
                chgb.inc(g.dma_start(out=bet[:, :], in_=e.hbm["l0a_beta"]), 16)
            blk.gpsimd(g_gb)
            m_gb = (chgb, chgb.mark())

            em = {}
            for tau in range(NT):
                for half in range(2):
                    kk = tau * 2 + half
                    po = ps_o[kk % 2]
                    def t_o(t, tau=tau, half=half, kk=kk, po=po):
                        _need(t, m_wo, m_z_all)
                        if ("hst", tau - 1) in em:
                            _need(t, em[("hst", tau - 1)])
                        for pr in range(8):
                            wo = e.wbig[pr // 4][:, (pr % 4) * 1024 + half * 512:(pr % 4) * 1024 + (half + 1) * 512]
                            ins = t.matmul(po[:, :],
                                           zall[:, pr * 2048 + tau * 128: pr * 2048 + (tau + 1) * 128],
                                           wo, start=(pr == 0), stop=(pr == 7))
                        e.t.inc(ins)
                    blk.tensor(t_o)
                    em[("omm", kk)] = (e.t, e.t.mark())

                def v_x(v, tau=tau):
                    if ("ln2", tau - 1) in em:
                        _need(v, em[("ln2", tau - 1)])
                    ins = v.tensor_scalar(out=xtm[:, :], in0=e.resid[:, tau * 1024:(tau + 1) * 1024],
                                          scalar1=e.rstd[:, tau:tau + 1],
                                          scalar2=e.lnb[:, tau:tau + 1],
                                          op0=ALU.mult, op1=ALU.add)
                    e.v.inc(ins)
                    v.drain()
                blk.vector(v_x)
                em[("xtm", tau)] = (e.v, e.v.mark())

                def v_h(v, tau=tau):
                    _need(v, em[("omm", tau * 2)], em[("omm", tau * 2 + 1)], em[("xtm", tau)])
                    if ("radd", tau - 1) in em:
                        _need(v, em[("radd", tau - 1)])
                    v.tensor_tensor(out=hb[:, 0:512], in0=ps_o[0][:, :],
                                    in1=xtm[:, 0:512], op=ALU.add)
                    v.tensor_tensor(out=hb[:, 512:1024], in0=ps_o[1][:, :],
                                    in1=xtm[:, 512:1024], op=ALU.add)
                    v.drain()
                    v.bn_stats(bn2[:, 0:6], hb[:, 0:512])
                    v.bn_stats(bn2[:, 6:12], hb[:, 512:1024])
                    v.drain()
                    bnv = bn2[:, :].rearrange("p (g k) -> p g k", k=3)
                    v.tensor_reduce(out=st2[:, 0:1], in_=bnv[:, :, 1], axis=AX.X, op=ALU.add)
                    v.tensor_reduce(out=st2[:, 1:2], in_=bnv[:, :, 2], axis=AX.X, op=ALU.add)
                    v.tensor_tensor(out=bnv[:, :, 0], in0=bnv[:, :, 1], in1=bnv[:, :, 1], op=ALU.mult)
                    v.drain()
                    v.tensor_reduce(out=st2[:, 2:3], in_=bnv[:, :, 0], axis=AX.X, op=ALU.add)
                    v.tensor_scalar_mul(out=st2[:, 0:1], in0=st2[:, 0:1], scalar1=0.25)
                    v.drain()
                    v.tensor_scalar_mul(out=st2[:, 2:3], in0=st2[:, 2:3], scalar1=0.25)
                    v.drain()
                    v.scalar_tensor_tensor(out=st2[:, 2:3], in0=st2[:, 1:2], scalar=1.0 / 1024,
                                           in1=st2[:, 2:3], op0=ALU.mult, op1=ALU.add)
                    v.drain()
                    v.tensor_tensor(out=st2[:, 1:2], in0=st2[:, 0:1], in1=st2[:, 0:1], op=ALU.mult)
                    v.drain()
                    v.tensor_tensor(out=st2[:, 2:3], in0=st2[:, 2:3], in1=st2[:, 1:2], op=ALU.subtract)
                    e.v.inc(v.drain())
                blk.vector(v_h)
                em[("hst", tau)] = (e.v, e.v.mark())

                def s_r2(s, tau=tau):
                    _need(s, em[("hst", tau)])
                    s.activation(out=st2[:, 2:3], in_=st2[:, 2:3], func=AF.Ln,
                                 bias=e.c_misc[:, 0:1], scale=1.0)
                    s.drain()
                    s.activation(out=st2[:, 2:3], in_=st2[:, 2:3], func=AF.Exp, scale=-0.5)
                    s.drain()
                    e.s.inc(s.drain())
                blk.scalar(s_r2)
                em[("r2", tau)] = (e.s, e.s.mark())

                def v_b2(v, tau=tau):
                    _need(v, em[("r2", tau)])
                    v.scalar_tensor_tensor(out=st2[:, 3:4], in0=st2[:, 0:1], scalar=-1.0,
                                           in1=st2[:, 2:3], op0=ALU.mult, op1=ALU.mult)
                    e.v.inc(v.drain())
                blk.vector(v_b2)
                em[("b2", tau)] = (e.v, e.v.mark())

                def v_ln2(v, tau=tau):
                    _need(v, em[("b2", tau)], m_gb)
                    ins = v.tensor_scalar(out=hb[:, :], in0=hb[:, :],
                                          scalar1=st2[:, 2:3], scalar2=st2[:, 3:4],
                                          op0=ALU.mult, op1=ALU.add)
                    e.v.inc(ins)
                    v.drain()
                blk.vector(v_ln2)
                em[("ln2", tau)] = (e.v, e.v.mark())

                def v_r(v, tau=tau):
                    _need(v, em[("ln2", tau)])
                    v.tensor_tensor(out=tb[:, :], in0=hb[:, :], in1=gam[:, :], op=ALU.mult)
                    v.drain()
                    v.tensor_tensor(out=tb[:, :], in0=tb[:, :], in1=bet[:, :], op=ALU.add)
                    v.drain()
                    ins = v.tensor_tensor(out=e.resid[:, tau * 1024:(tau + 1) * 1024],
                                          in0=e.resid[:, tau * 1024:(tau + 1) * 1024],
                                          in1=tb[:, :], op=ALU.add)
                    e.v.inc(ins)
                    v.drain()
                blk.vector(v_r)
                em[("radd", tau)] = (e.v, e.v.mark())

        e.m_block_end = [em[("radd", NT - 1)]]
        _emit_join(e)


def _emit_l1a(e, first):
    nc, blk = e.nc, e.blk
    with ExitStack() as ph:
        sb = lambda n, sh, dt: ph.enter_context(nc.sbuf_tensor(n, sh, dt))
        e.wsml = [sb(f"l1a_ws{i}", [128, 1024], BF16) for i in range(4)]
        e.wbig = [sb(f"l1a_wb{i}", [128, 4096], BF16) for i in range(2)]
        qbuf = sb("l1a_q", [128, NC_ * 2048], BF16)
        kbuf = sb("l1a_k", [128, NC_ * 2048], BF16)
        vtm = sb("l1a_v", [128, NT * 1040], BF16)
        b8 = sb("l1a_b8", [128, 16], F32)
        msk = sb("l1a_msk", [128, 144], BF16)

        sm = _emit_ln_stats(e, {"resid": (e.v, e.v.mark())})

        wm, pm, am, om = {}, {}, {}, {}

        chp = _gch(e, "l1a_pre")
        def g_pre(g):
            chp.inc(g.dma_start(out=b8[:, :], in_=e.hbm["l1a_bqk"]), 16)
            chp.inc(g.dma_start(out=msk[:, :], in_=e.hbm["c_l1mask"]), 16)
        blk.gpsimd(g_pre)
        m_pre = (chp, chp.mark())

        def v_b8(v):
            _need(v, m_pre)
            v.tensor_scalar_mul(out=b8[:, 0:8], in0=b8[:, 0:8], scalar1=0.125)
            e.v.inc(v.drain())
        blk.vector(v_b8)
        m_b8 = (e.v, e.v.mark())

        def v_ones(v):
            ins = v.memset(vtm[:, :].rearrange("p (t h c) -> p t h c", h=16, c=65)[:, :, :, 64:65], 1)
            e.v.inc(ins)
        blk.vector(v_ones)
        m_ones = (e.v, e.v.mark())

        def load_wqk(sg, o):
            k = sg * 16 + o
            ch = _gch(e, f"l1a_wqk{k % 2}")
            def g_w(g, ch=ch):
                if ("wqku", k - 2) in pm:
                    _need(g, pm[("wqku", k - 2)])
                ins = g.dma_start(out=e.wsml[(k % 2) * 2][:, :],
                                  in_=e.hbm["l1a_wqk"][o].rearrange("p c j -> p (c j)"))
                ch.inc(ins, 16)
            blk.gpsimd(g_w)
            wm[("wqk", k)] = (ch, ch.mark())

        chv = _gch(e, "l1a_wv")
        def g_wv(g):
            for c in range(8):
                ins = g.dma_start(out=e.wbig[c // 4][:, (c % 4) * 1024:(c % 4 + 1) * 1024],
                                  in_=e.hbm["l1a_wv"][c])
                chv.inc(ins, 16)
        blk.gpsimd(g_wv)
        m_wv = (chv, chv.mark())

        projscope = ExitStack()
        ps_p = [projscope.enter_context(nc.psum_tensor(f"l1a_pp{i}", [128, 512], F32)) for i in range(2)]
        ps_v = [projscope.enter_context(nc.psum_tensor(f"l1a_pv{i}", [128, 1024], F32)) for i in range(2)]

        load_wqk(0, 0)
        load_wqk(0, 1)
        bv_done = {}
        for sg in range(4):
            with ExitStack() as sph:
                xn = sph.enter_context(nc.sbuf_tensor(f"l1a_xn{sg}", [128, NC_ * 512], BF16))
                lnm = _emit_ln_apply(e, sph, xn, 0, sg * 4, 4, f"l1a{sg}", sm)
                for o in range(16):
                    dstbuf = qbuf if o < 8 else kbuf
                    oc = o % 8
                    pp = ps_p[o % 2]
                    kq = sg * 16 + o
                    def t_qk(t, sg=sg, o=o, kq=kq, pp=pp):
                        _need(t, wm[("wqk", kq)], lnm["fm"])
                        if ("qkev", sg, o - 2) in pm:
                            _need(t, pm[("qkev", sg, o - 2)])
                        elif sg > 0 and o < 2:
                            _need(t, pm[("qkev", sg - 1, 14 + o)])
                        wq = e.wsml[(kq % 2) * 2]
                        for c in range(NC_):
                            ins = t.matmul(pp[:, :], wq[:, c * 128:(c + 1) * 128],
                                           xn[:, c * 512:(c + 1) * 512],
                                           start=(c == 0), stop=(c == 7))
                        e.t.inc(ins)
                    blk.tensor(t_qk)
                    pm[("qkmm", sg, o)] = (e.t, e.t.mark())
                    pm[("wqku", kq)] = (e.t, e.t.mark())
                    if kq + 2 < 64:
                        load_wqk((kq + 2) // 16, (kq + 2) % 16)

                    def s_qk(s, sg=sg, o=o, oc=oc, dstbuf=dstbuf, pp=pp):
                        _need(s, pm[("qkmm", sg, o)], m_b8)
                        ins = s.activation(out=dstbuf[:, oc * 2048 + sg * 512: oc * 2048 + (sg + 1) * 512],
                                           in_=pp[:, :], func=AF.Identity,
                                           scale=0.125 if o < 8 else 1.0,
                                           bias=b8[:, o:o + 1])
                        e.s.inc(ins)
                    blk.scalar(s_qk)
                    pm[("qkev", sg, o)] = (e.s, e.s.mark())

                for tt_i in range(4):
                    tau = sg * 4 + tt_i
                    pv = ps_v[tt_i % 2]
                    def t_v(t, sg=sg, tt_i=tt_i, tau=tau, pv=pv):
                        _need(t, m_wv, lnm["fm"])
                        if ("vev", sg, tt_i - 2) in pm:
                            _need(t, pm[("vev", sg, tt_i - 2)])
                        elif sg > 0 and tt_i < 2:
                            _need(t, pm[("vev", sg - 1, 2 + tt_i)])
                        for c in range(NC_):
                            wv = e.wbig[c // 4][:, (c % 4) * 1024:(c % 4 + 1) * 1024]
                            t.matmul(pv[:, 0:512], xn[:, c * 512 + tt_i * 128: c * 512 + (tt_i + 1) * 128],
                                     wv[:, 0:512], start=(c == 0), stop=(c == 7))
                        for c in range(NC_):
                            wv = e.wbig[c // 4][:, (c % 4) * 1024:(c % 4 + 1) * 1024]
                            ins = t.matmul(pv[:, 512:1024], xn[:, c * 512 + tt_i * 128: c * 512 + (tt_i + 1) * 128],
                                           wv[:, 512:1024], start=(c == 0), stop=(c == 7))
                        e.t.inc(ins)
                    blk.tensor(t_v)
                    pm[("vmm", sg, tt_i)] = (e.t, e.t.mark())

                    def s_v(s, tau=tau, sg=sg, tt_i=tt_i, pv=pv):
                        _need(s, pm[("vmm", sg, tt_i)], m_ones)
                        vv = vtm[:, tau * 1040:(tau + 1) * 1040].rearrange("p (h c) -> p h c", c=65)
                        ins = s.copy(out=vv[:, :, 0:64],
                                     in_=pv[:, :].rearrange("p (h c) -> p h c", c=64))
                        e.s.inc(ins)
                    blk.scalar(s_v)
                    pm[("vev", sg, tt_i)] = (e.s, e.s.mark())
        pm[("wqku_all",)] = pm[("qkev", 3, 15)]
        projscope.close()

        with ExitStack() as bvph:
            bvr = bvph.enter_context(nc.sbuf_tensor("l1a_bvr", [128, 1040], F32))
            chb = _gch(e, "l1a_bv")
            def g_bv(g):
                _need(g, pm[("vev", 3, 3)])
                chb.inc(g.dma_start(out=bvr[:, :], in_=e.hbm["l1a_bv"]), 16)
            blk.gpsimd(g_bv)
            m_bv = (chb, chb.mark())
            for tau in range(NT):
                def v_bv(v, tau=tau):
                    _need(v, m_bv, pm[("vev", tau // 4, tau % 4)])
                    ins = v.tensor_tensor(out=vtm[:, tau * 1040:(tau + 1) * 1040],
                                          in0=vtm[:, tau * 1040:(tau + 1) * 1040],
                                          in1=bvr[:, :], op=ALU.add)
                    e.v.inc(ins)
                blk.vector(v_bv)
                bv_done[tau] = (e.v, e.v.mark())
            m_v_ready = (e.v, e.v.mark())

            cho = _gch(e, "l1a_wo")
            def g_wo(g):
                _need(g, pm[("vmm", 3, 3)])
                for c in range(8):
                    ins = g.dma_start(out=e.wbig[c // 4][:, (c % 4) * 1024:(c % 4 + 1) * 1024],
                                      in_=e.hbm["l1a_wo"][c])
                    cho.inc(ins, 16)
            blk.gpsimd(g_wo)
            m_wo = (cho, cho.mark())

        with ExitStack() as aph:
            pbuf = aph.enter_context(nc.sbuf_tensor("l1a_pt", [128, 3 * 1024], BF16))
            cst = aph.enter_context(nc.sbuf_tensor("l1a_cst", [128, 512], BF16))
            stg = aph.enter_context(nc.sbuf_tensor("l1a_stg", [64, 512], BF16))
            ctxb = aph.enter_context(nc.sbuf_tensor("l1a_ctx", [128, 8 * 512], BF16))
            rcp = aph.enter_context(nc.sbuf_tensor("l1a_rcp", [128, 512], F32))
            conso = aph.enter_context(nc.sbuf_tensor("l1a_on", [128, 64], F32))
            ps_s2 = [aph.enter_context(nc.psum_tensor(f"l1a_ps{i}", [128, 1024], F32)) for i in range(2)]
            ps_c = [aph.enter_context(nc.psum_tensor(f"l1a_pc{i}", [128, 512], F32)) for i in range(2)]
            ps_r = aph.enter_context(nc.psum_tensor("l1a_pr", [128, 512], F32))

            chc = _gch(e, "l1a_on")
            def g_bo(g):
                _need(g, m_v_ready)
                chc.inc(g.dma_start(out=conso[:, :], in_=e.hbm["c_ones64"]), 16)
            blk.gpsimd(g_bo)
            m_bo = (chc, chc.mark())

            with ExitStack() as boph:
                bor = boph.enter_context(nc.sbuf_tensor("l1a_bor", [128, 1024], BF16))
                chb2 = _gch(e, "l1a_bo")
                def g_bo2(g):
                    _need(g, m_v_ready)
                    chb2.inc(g.dma_start(out=bor[:, :], in_=e.hbm["l1a_bo"]), 16)
                blk.gpsimd(g_bo2)
                m_bo2 = (chb2, chb2.mark())
                for tau in range(NT):
                    def v_bo(v, tau=tau):
                        _need(v, m_bo2)
                        if tau == 0:
                            _need(v, sm, pm[("wqku_all",)], pm[("vev", 3, 3)])
                        ins = v.tensor_tensor(out=e.resid[:, tau * 1024:(tau + 1) * 1024],
                                              in0=e.resid[:, tau * 1024:(tau + 1) * 1024],
                                              in1=bor[:, :], op=ALU.add)
                        e.v.inc(ins)
                    blk.vector(v_bo)
                m_bo_add = (e.v, e.v.mark())

            for sg in range(4):
                q0 = sg * 512
                pend_tail = [None]
                for h in range(16):
                    hrow = (h % 2) * 64
                    hcolq = (h // 2) * 2048

                    def scp(t, P, sg=sg, h=h, hrow=hrow, hcolq=hcolq):
                        _need(t, pm[("wqku_all",)])
                        if sg == 0 and h == 0 and P < 2:
                            _need(t, pm[("vev", 3, 3)])
                        if ("exp", sg, h, P - 2) in am:
                            _need(t, am[("exp", sg, h, P - 2)])
                        elif P < 2 and ("exp", sg, h - 1, 6 + P) in am:
                            _need(t, am[("exp", sg, h - 1, 6 + P)])

                        if P < 2 and h == 0 and sg > 0:
                            _need(t, om[("oev", sg - 1, 7)])
                        for half in range(2):
                            kt = 2 * P + half
                            ins = t.matmul(ps_s2[P % 2][:, half * 512:(half + 1) * 512],
                                           kbuf[hrow:hrow + 64, hcolq + kt * 128: hcolq + (kt + 1) * 128],
                                           qbuf[hrow:hrow + 64, hcolq + q0: hcolq + q0 + 512],
                                           start=True, stop=True)
                        e.t.inc(ins)
                        am[("sc", sg, h, P)] = (e.t, e.t.mark())

                    def mskp(v, P, sg=sg, h=h):
                        any_ = False
                        ins = None
                        for half in range(2):
                            kt = 2 * P + half
                            k0 = kt * 128
                            qlo, qhi = max(k0, q0), min(k0 + 144, q0 + 512)
                            if qlo < qhi:
                                if not any_:
                                    _need(v, am[("sc", sg, h, P)], m_pre)
                                    any_ = True
                                ins = v.tensor_tensor(
                                    out=ps_s2[P % 2][:, half * 512 + qlo - q0: half * 512 + qhi - q0],
                                    in0=ps_s2[P % 2][:, half * 512 + qlo - q0: half * 512 + qhi - q0],
                                    in1=msk[:, qlo - k0:qhi - k0], op=ALU.add)
                        if any_:
                            e.v.inc(ins)
                            am[("msk", sg, h, P)] = (e.v, e.v.mark())

                    def expp(s, P, sg=sg, h=h):
                        _need(s, am.get(("msk", sg, h, P), am[("sc", sg, h, P)]))
                        if ("pv", sg, h, P - 3) in am:
                            _need(s, am[("pv", sg, h, P - 3)])
                        elif P < 3 and ("pv", sg, h - 1, 5 + P) in am:
                            _need(s, am[("pv", sg, h - 1, 5 + P)])
                        elif P < 3 and h == 0 and sg > 0 and ("pv", sg - 1, 15, 5 + P) in am:
                            _need(s, am[("pv", sg - 1, 15, 5 + P)])
                        ins = s.activation(out=pbuf[:, (P % 3) * 1024:(P % 3 + 1) * 1024],
                                           in_=ps_s2[P % 2][:, :], func=AF.Exp)
                        e.s.inc(ins)
                        am[("exp", sg, h, P)] = (e.s, e.s.mark())

                    def pvp(t, P, sg=sg, h=h):
                        psc = ps_c[h % 2]
                        _need(t, am[("exp", sg, h, P)], m_v_ready)
                        if P == 0:
                            if ("ctt", sg, h - 2) in am:
                                _need(t, am[("ctt", sg, h - 2)])
                            elif sg > 0 and h < 2:
                                _need(t, am[("ctt", sg - 1, 14 + h)])
                        for half in range(2):
                            kt = 2 * P + half
                            ins = t.matmul(psc[0:65, :],
                                           vtm[:, kt * 1040 + h * 65: kt * 1040 + (h + 1) * 65],
                                           pbuf[:, (P % 3) * 1024 + half * 512: (P % 3) * 1024 + (half + 1) * 512],
                                           start=(kt == 0), stop=(kt == 15))
                        e.t.inc(ins)
                        am[("pv", sg, h, P)] = (e.t, e.t.mark())

                    for P in range(8):
                        blk.tensor(lambda t, P=P: scp(t, P))
                        blk.vector(lambda v, P=P: mskp(v, P))
                        blk.scalar(lambda s, P=P: expp(s, P))
                        if P == 5 and pend_tail[0] is not None:
                            pend_tail[0]()
                            pend_tail[0] = None
                        if P >= 2:
                            blk.tensor(lambda t, P=P: pvp(t, P - 2))
                    blk.tensor(lambda t: pvp(t, 6))
                    blk.tensor(lambda t: pvp(t, 7))

                    def v_rcp(v, sg=sg, h=h, psc=ps_c[h % 2]):
                        _need(v, am[("pv", sg, h, 7)])
                        if ("rep", sg, h - 1) in am:
                            _need(v, am[("rep", sg, h - 1)])
                        elif sg > 0 and h == 0:
                            _need(v, am[("rep", sg - 1, 15)])
                        ins = v.reciprocal(out=rcp[64:65, :], in_=psc[64:65, :])
                        e.v.inc(ins)
                    blk.vector(v_rcp)
                    am[("rcp", sg, h)] = (e.v, e.v.mark())

                    def v_cst(v, sg=sg, h=h, psc=ps_c[h % 2]):
                        _need(v, am[("pv", sg, h, 7)])
                        if ("ctt", sg, h - 1) in am:
                            _need(v, am[("ctt", sg, h - 1)])
                        elif sg > 0 and h == 0:
                            _need(v, am[("ctt", sg - 1, 15)])
                        ins = v.tensor_copy(out=cst[0:64, :], in_=psc[0:64, :])
                        e.v.inc(ins)
                    blk.vector(v_cst)
                    am[("cst", sg, h)] = (e.v, e.v.mark())

                    def tail(sg=sg, h=h):
                        def t_rep(t, sg=sg, h=h):
                            _need(t, am[("rcp", sg, h)], m_bo)
                            if ("ctt", sg, h - 1) in am:
                                _need(t, am[("ctt", sg, h - 1)])
                            elif sg > 0 and h == 0:
                                _need(t, am[("ctt", sg - 1, 15)])
                            ins = t.matmul(ps_r[0:64, :], conso[64:65, :], rcp[64:65, :],
                                           start=True, stop=True)
                            e.t.inc(ins)
                        blk.tensor(t_rep)
                        am[("rep", sg, h)] = (e.t, e.t.mark())

                        if h % 2 == 0:
                            def v_ctt(v, sg=sg, h=h):
                                _need(v, am[("rep", sg, h)], am[("cst", sg, h)])
                                if sg > 0 and h == 0 and ("oev", sg - 1, 7) in om:
                                    _need(v, om[("oev", sg - 1, 7)])
                                ins = v.tensor_tensor(out=ctxb[0:64, (h // 2) * 512:(h // 2 + 1) * 512],
                                                      in0=cst[0:64, :], in1=ps_r[0:64, :], op=ALU.mult)
                                e.v.inc(ins)
                            blk.vector(v_ctt)
                            am[("ctt", sg, h)] = (e.v, e.v.mark())
                        else:
                            def v_ctt(v, sg=sg, h=h):
                                _need(v, am[("rep", sg, h)], am[("cst", sg, h)])
                                if ("stgdma", sg, h - 2) in am:
                                    _need(v, am[("stgdma", sg, h - 2)])
                                elif sg > 0 and h == 1:
                                    _need(v, am[("stgdma", sg - 1, 15)])
                                ins = v.tensor_tensor(out=stg[:, :], in0=cst[0:64, :],
                                                      in1=ps_r[0:64, :], op=ALU.mult)
                                e.v.inc(ins)
                            blk.vector(v_ctt)
                            am[("ctt", sg, h)] = (e.v, e.v.mark())

                            def y_stg(y, sg=sg, h=h):
                                _need(y, am[("ctt", sg, h)])
                                if ("oev", sg - 1, 7) in om and h == 1:
                                    _need(y, om[("oev", sg - 1, 7)])
                                ins = y.dma_start(out=ctxb[64:128, (h // 2) * 512:(h // 2 + 1) * 512],
                                                  in_=stg[:, :])
                                e.y.inc(ins, 16)
                            blk.sync(y_stg)
                            am[("stgdma", sg, h)] = (e.y, e.y.mark())
                    pend_tail[0] = tail

                if pend_tail[0] is not None:
                    pend_tail[0]()
                    pend_tail[0] = None

                for tt_i in range(4):
                    tau = sg * 4 + tt_i
                    for half in range(2):
                        kk = tt_i * 2 + half
                        po = ps_s2[kk % 2][:, 0:512]
                        def t_o(t, sg=sg, tt_i=tt_i, half=half, kk=kk, po=po):
                            _need(t, m_wo, am[("exp", sg, 15, 7)])
                            for pr in range(8):
                                if tt_i == 0 and half == 0:
                                    _need(t, am[("ctt", sg, pr * 2)], am[("stgdma", sg, pr * 2 + 1)])
                            if ("oev", sg, kk - 2) in om:
                                _need(t, om[("oev", sg, kk - 2)])
                            elif sg > 0 and kk < 2:
                                _need(t, om[("oev", sg - 1, 6 + kk)])
                            for pr in range(8):
                                wo = e.wbig[pr // 4][:, (pr % 4) * 1024 + half * 512:(pr % 4) * 1024 + (half + 1) * 512]
                                ins = t.matmul(po,
                                               ctxb[:, pr * 512 + tt_i * 128: pr * 512 + (tt_i + 1) * 128],
                                               wo, start=(pr == 0), stop=(pr == 7))
                            e.t.inc(ins)
                        blk.tensor(t_o)
                        om[("omm", sg, kk)] = (e.t, e.t.mark())

                        def v_oadd(v, sg=sg, tau=tau, half=half, kk=kk, po=po):
                            _need(v, om[("omm", sg, kk)], m_bo_add)
                            dst = e.resid[:, tau * 1024 + half * 512: tau * 1024 + (half + 1) * 512]
                            ins = v.tensor_tensor(out=dst, in0=dst, in1=po, op=ALU.add)
                            e.v.inc(ins)
                        blk.vector(v_oadd)
                        om[("oev", sg, kk)] = (e.v, e.v.mark())

        e.m_block_end = [om[("oev", 3, 7)]]
        _emit_join(e)


def _emit_l1f(e, first):
    nc, blk = e.nc, e.blk
    NSL, TSL = 2, 1024
    NTS = TSL // 128
    with ExitStack() as ph:
        sb = lambda n, sh, dt: ph.enter_context(nc.sbuf_tensor(n, sh, dt))

        sm = _emit_ln_stats(e, {"resid": (e.v, e.v.mark())})

        def y_spill(y):
            _need(y, sm)
            for t in range(NT):
                ins = y.dma_start(out=e.spill[t * 128:(t + 1) * 128, :],
                                  in_=e.resid[:, t * 1024:(t + 1) * 1024])
                e.y.inc(ins, 16)
        blk.sync(y_spill)
        m_spill = (e.y, e.y.mark())

        e.resid_cm.__exit__(None, None, None)
        h1o_cm = nc.sbuf_tensor("l1f_h1o", [128, 32 * 1024], BF16, side="right")
        h1o = h1o_cm.__enter__()

        e.wsml = [sb(f"l1f_ws{i}", [128, 1024], BF16) for i in range(4)]
        e.wbig = [sb(f"l1f_wb{i}", [128, 4096], BF16) for i in range(2)]
        gbuf = sb("l1f_g", [128, 32 * TSL], BF16)
        t1 = sb("l1f_t1", [128, 512], BF16)
        t2 = [sb(f"l1f_t2{i}", [128, 512], BF16) for i in range(2)]
        scl = sb("l1f_scl", [128, 4 * 32], F32)
        ps_a = [ph.enter_context(nc.psum_tensor(f"l1f_pa{i}", [128, 512], F32)) for i in range(4)]
        ps_o = [ph.enter_context(nc.psum_tensor(f"l1f_po{i}", [128, 512], F32)) for i in range(2)]

        chs = _gch(e, "l1f_scl")
        def g_scales(g):
            for i, k in enumerate(("l1f_s1", "l1f_s2", "l1f_sh1", "l1f_sh2")):
                chs.inc(g.dma_start(out=scl[:, i * 32:(i + 1) * 32], in_=e.hbm[k]), 16)
        blk.gpsimd(g_scales)
        m_scl = (chs, chs.mark())

        wm, mm_m, ev_m, tt_m, out_m = {}, {}, {}, {}, {}

        def load_w12(k):
            o, sl = k % 32, k // 32
            ch = _gch(e, f"l1f_w12{k % 2}")
            def g_w(g, ch=ch):
                if ("w12u", k - 2) in mm_m:
                    _need(g, mm_m[("w12u", k - 2)])
                ins = g.dma_start(out=e.wsml[(k % 2) * 2][:, :],
                                  in_=e.hbm["l1f_w1"][o].rearrange("p c j -> p (c j)"))
                ch.inc(ins, 16)
                ins = g.dma_start(out=e.wsml[(k % 2) * 2 + 1][:, :],
                                  in_=e.hbm["l1f_w2"][o].rearrange("p c j -> p (c j)"))
                ch.inc(ins, 16)
            blk.gpsimd(g_w)
            wm[("w12", k)] = (ch, ch.mark())

        def load_h(which, k, extra=None):
            o = k % 32
            ch = _gch(e, f"l1f_h{k % 2}")
            def g_w(g, ch=ch):
                if (which + "u", k - 2) in mm_m:
                    _need(g, mm_m[(which + "u", k - 2)])
                if extra is not None:
                    _need(g, extra)
                ins = g.dma_start(out=e.wbig[k % 2][:, :],
                                  in_=e.hbm["l1f_" + which][o].rearrange("p c j -> p (c j)"))
                ch.inc(ins, 16)
            blk.gpsimd(g_w)
            wm[(which, k)] = (ch, ch.mark())

        load_w12(0)
        load_w12(1)

        for sl in range(NSL):
            with ExitStack() as slph:
                xnscope = ExitStack()
                xn = xnscope.enter_context(nc.sbuf_tensor(f"l1f_xn{sl}", [128, NC_ * TSL], BF16))
                lnm = _emit_ln_apply_from_spill(e, xnscope, xn, sl * NTS, NTS,
                                                f"l1f{sl}", sm, m_spill)

                for o in range(32):
                    k = sl * 32 + o
                    wg1, wg2 = e.wsml[(k % 2) * 2], e.wsml[(k % 2) * 2 + 1]
                    for sg in range(2):
                        j = k * 2 + sg
                        pga, pgb = ps_a[(j % 2) * 2], ps_a[(j % 2) * 2 + 1]
                        def t_mm(t, sl=sl, o=o, sg=sg, k=k, j=j, wg1=wg1, wg2=wg2, pga=pga, pgb=pgb):
                            _need(t, wm[("w12", k)], lnm["fm"])
                            if ("gdone", j - 2) in ev_m:
                                _need(t, ev_m[("gdone", j - 2)])
                            if ("gtt", j - 2) in tt_m:
                                _need(t, tt_m[("gtt", j - 2)])
                            if ("hdone", sl - 1, 1, 63) in ev_m and o == 0 and sg == 0:
                                _need(t, ev_m[("hdone", sl - 1, 1, 63)])
                            if sl > 0 and o == 0 and sg == 0:
                                _need(t, out_m[("ovlast", sl - 1)])
                            col0 = sg * 512
                            for c in range(NC_):
                                t.matmul(pga[:, :], wg1[:, c * 128:(c + 1) * 128],
                                         xn[:, c * TSL + col0: c * TSL + col0 + 512],
                                         start=(c == 0), stop=(c == 7))
                            for c in range(NC_):
                                ins = t.matmul(pgb[:, :], wg2[:, c * 128:(c + 1) * 128],
                                               xn[:, c * TSL + col0: c * TSL + col0 + 512],
                                               start=(c == 0), stop=(c == 7))
                            e.t.inc(ins)
                        blk.tensor(t_mm)
                        mm_m[("gmmj", j)] = (e.t, e.t.mark())
                        if sg == 1:
                            mm_m[("w12u", k)] = (e.t, e.t.mark())
                            if k + 2 < NSL * 32:
                                load_w12(k + 2)

                        def s_ev(s, sl=sl, o=o, sg=sg, j=j, pga=pga):
                            _need(s, mm_m[("gmmj", j)], m_scl)
                            if ("gtt", j - 1) in tt_m:
                                _need(s, tt_m[("gtt", j - 1)])
                            ins = s.activation(out=t1[:, :], in_=pga[:, :], func=AF.Silu,
                                               scale=scl[:, o:o + 1])
                            e.s.inc(ins)
                        blk.scalar(s_ev)
                        ev_m[("gdone", j)] = (e.s, e.s.mark())

                        def v_tt(v, sl=sl, o=o, sg=sg, j=j, pgb=pgb):
                            _need(v, ev_m[("gdone", j)], m_scl)
                            if ("gtt", j - 2) in tt_m:
                                _need(v, tt_m[("gtt", j - 2)])
                            if sl > 0 and o == 0 and sg == 0:
                                _need(v, out_m[("ovlast", sl - 1)])
                            v.tensor_scalar_mul(out=t2[j % 2][:, :], in0=pgb[:, :],
                                                scalar1=scl[:, 32 + o:32 + o + 1])
                            v.drain()
                            ins = v.tensor_tensor(out=gbuf[:, o * TSL + sg * 512: o * TSL + (sg + 1) * 512],
                                                  in0=t1[:, :], in1=t2[j % 2][:, :], op=ALU.mult)
                            e.v.inc(ins)
                        blk.vector(v_tt)
                        tt_m[("gtt", j)] = (e.v, e.v.mark())

                xnscope.close()

                for stage in range(2):
                    wkey = ("h1", "h2")[stage]
                    srcbuf = (gbuf, h1o)[stage]
                    dstbuf = (h1o, gbuf)[stage]
                    scol = (64, 96)[stage]
                    src_ready = tt_m[("gtt", sl * 64 + 63)] if stage == 0 else \
                        ev_m[("hdone", sl, 0, 63)]
                    for kk0 in range(2):
                        prev_user = None
                        if stage == 1:
                            prev_user = mm_m[("h1u", sl * 32 + 30 + kk0)]
                        elif sl > 0:
                            prev_user = mm_m[("h2u", (sl - 1) * 32 + 30 + kk0)]
                        load_h(wkey, sl * 32 + kk0, extra=prev_user)
                    for o in range(32):
                        k = sl * 32 + o
                        wgt = e.wbig[k % 2]
                        for sg in range(2):
                            j = o * 2 + sg
                            pp = ps_a[(j % 2) * 2 + stage]
                            def t_hmm(t, stage=stage, sl=sl, o=o, sg=sg, k=k, j=j,
                                      wgt=wgt, srcbuf=srcbuf, wkey=wkey, pp=pp,
                                      src_ready=src_ready):
                                _need(t, wm[(wkey, k)], src_ready)
                                if ("hdone", sl, stage, j - 2) in ev_m:
                                    _need(t, ev_m[("hdone", sl, stage, j - 2)])
                                if stage == 0 and o == 0 and sg == 0 and sl == 0:
                                    _need(t, m_spill)
                                col0 = sg * 512
                                for c in range(32):
                                    ins = t.matmul(pp[:, :], wgt[:, c * 128:(c + 1) * 128],
                                                   srcbuf[:, c * TSL + col0: c * TSL + col0 + 512],
                                                   start=(c == 0), stop=(c == 31))
                                e.t.inc(ins)
                            blk.tensor(t_hmm)
                            mm_m[("hmm", sl, stage, j)] = (e.t, e.t.mark())
                            if sg == 1:
                                mm_m[(wkey + "u", k)] = (e.t, e.t.mark())
                                if o + 2 < 32:
                                    load_h(wkey, k + 2)

                            def s_hev(s, stage=stage, sl=sl, o=o, sg=sg, j=j,
                                      dstbuf=dstbuf, scol=scol, pp=pp):
                                _need(s, mm_m[("hmm", sl, stage, j)])
                                ins = s.activation(out=dstbuf[:, o * TSL + sg * 512: o * TSL + (sg + 1) * 512],
                                                   in_=pp[:, :], func=AF.Gelu,
                                                   scale=scl[:, scol + o:scol + o + 1])
                                e.s.inc(ins)
                            blk.scalar(s_hev)
                            ev_m[("hdone", sl, stage, j)] = (e.s, e.s.mark())

                h2_ready = ev_m[("hdone", sl, 1, 63)]
                with ExitStack() as wph:
                    w3b = wph.enter_context(nc.sbuf_tensor(f"l1f_w3b{sl}", [128, 16 * 1024], BF16))
                    rq = wph.enter_context(nc.sbuf_tensor(f"l1f_rq{sl}", [128, 4 * 1024], F32))
                    s3r = wph.enter_context(nc.sbuf_tensor(f"l1f_s3r{sl}", [128, 1024], BF16))

                    ch3 = _gch(e, "l1f_s3")
                    def g_s3(g, ch3=ch3, s3r=s3r, sl=sl):
                        if sl > 0:
                            _need(g, out_m[("ovlast", sl - 1)])
                        ch3.inc(g.dma_start(out=s3r[:, :], in_=e.hbm["l1f_s3"]), 16)
                    blk.gpsimd(g_s3)
                    m_s3 = (ch3, ch3.mark())

                    def load_w3(hs, cpass, guard):
                        def g_w(g, hs=hs, cpass=cpass):
                            _need(g, guard)
                            for ci in range(16):
                                chw = _gch(e, f"l1f_w3c{ci}")
                                ins = g.dma_start(out=w3b[:, ci * 1024:(ci + 1) * 1024],
                                                  in_=e.hbm["l1f_w3"][cpass * 16 + ci])
                                chw.inc(ins, 16)
                                wm[("w3c", sl, hs, cpass, ci)] = (chw, chw.mark())
                        blk.gpsimd(g_w)

                    for hs in range(2):
                        def y_rq(y, sl=sl, hs=hs):
                            _need(y, m_spill)
                            if ("outdma", sl, hs - 1) in out_m:
                                _need(y, out_m[("outdma", sl, hs - 1)])
                            elif sl > 0 and hs == 0:
                                _need(y, out_m[("outdma", sl - 1, 1)])
                            r0 = sl * 1024 + hs * 512
                            ins = y.dma_start(
                                out=rq[:, :].rearrange("p (t f) -> p t f", f=1024),
                                in_=e.spill[r0:r0 + 512, :].rearrange("(t p) f -> p t f", p=128))
                            e.y.inc(ins, 16)
                        blk.sync(y_rq)
                        out_m[("rq", sl, hs)] = (e.y, e.y.mark())

                        for cpass in range(2):
                            key_prev = ("w3u", sl, hs, cpass - 1) if cpass else ("w3u", sl, hs - 1, 1)
                            if sl == 0 and hs == 0 and cpass == 0:
                                guard = (e.t, 0)
                            elif key_prev in wm:
                                guard = wm[key_prev]
                            else:
                                guard = wm[("w3u", sl - 1, 1, 1)]
                            load_w3(hs, cpass, guard)
                            for tt_i in range(4):
                                ti = hs * 4 + tt_i
                                for half in range(2):
                                    kk = tt_i * 2 + half
                                    po = ps_o[kk % 2]
                                    def t_w3(t, sl=sl, hs=hs, cpass=cpass, ti=ti, half=half, kk=kk, po=po):
                                        _need(t, h2_ready, out_m[("rq", sl, hs)])
                                        if ("ov", sl, hs, cpass, kk - 2) in out_m:
                                            _need(t, out_m[("ov", sl, hs, cpass, kk - 2)])
                                        elif cpass == 1 and ("ov", sl, hs, 0, 6 + kk) in out_m:
                                            _need(t, out_m[("ov", sl, hs, 0, 6 + kk)])
                                        elif hs == 1 and cpass == 0 and ("ov", sl, 0, 1, 6 + kk) in out_m:
                                            _need(t, out_m[("ov", sl, 0, 1, 6 + kk)])
                                        for ci in range(16):
                                            _need(t, wm[("w3c", sl, hs, cpass, ci)])
                                            ins = t.matmul(po[:, :],
                                                           gbuf[:, (cpass * 16 + ci) * TSL + ti * 128:
                                                                (cpass * 16 + ci) * TSL + (ti + 1) * 128],
                                                           w3b[:, ci * 1024 + half * 512: ci * 1024 + (half + 1) * 512],
                                                           start=(ci == 0), stop=(ci == 15))
                                        e.t.inc(ins)
                                    blk.tensor(t_w3)
                                    out_m[("omm", sl, hs, cpass, kk)] = (e.t, e.t.mark())

                                    def v_out(v, sl=sl, hs=hs, cpass=cpass, tt_i=tt_i, half=half, kk=kk, po=po):
                                        _need(v, out_m[("omm", sl, hs, cpass, kk)], m_s3)
                                        dst = rq[:, tt_i * 1024 + half * 512: tt_i * 1024 + (half + 1) * 512]
                                        if ("ov", sl, hs, cpass, kk - 2) in out_m:
                                            _need(v, out_m[("ov", sl, hs, cpass, kk - 2)])
                                        v.tensor_tensor(out=t2[kk % 2][:, :], in0=po[:, :],
                                                        in1=s3r[:, half * 512:(half + 1) * 512], op=ALU.mult)
                                        v.drain()
                                        ins = v.tensor_tensor(out=dst, in0=dst, in1=t2[kk % 2][:, :], op=ALU.add)
                                        e.v.inc(ins)
                                    blk.vector(v_out)
                                    out_m[("ov", sl, hs, cpass, kk)] = (e.v, e.v.mark())
                            wm[("w3u", sl, hs, cpass)] = out_m[("omm", sl, hs, cpass, 7)]
                        out_m[("ovlast_h", sl, hs)] = out_m[("ov", sl, hs, 1, 7)]

                        cho2 = _gch(e, "l1f_out")
                        def y_out(y, sl=sl, hs=hs, cho2=cho2):
                            _need(y, out_m[("ovlast_h", sl, hs)])
                            r0 = sl * 1024 + hs * 512
                            ins = y.dma_start(
                                out=e.out[r0:r0 + 512, :].rearrange("(t p) f -> p t f", p=128),
                                in_=rq[:, :].rearrange("p (t f) -> p t f", f=1024))
                            cho2.inc(ins, 16)
                        blk.sync(y_out)
                        out_m[("outdma", sl, hs)] = (cho2, cho2.mark())
                    out_m[("ovlast", sl)] = out_m[("ovlast_h", sl, 1)]

        def y_fin(y):
            y.wait_ge(e.y.sem, e.y.mark())
            cho2 = _gch(e, "l1f_out")
            y.wait_ge(cho2.sem, cho2.mark())
        blk.sync(y_fin)
        _emit_join(e)



def kernel(**inputs):
    consts = _host_consts()
    e = build_nc()
    in_maps = []
    for core in range(8):
        d = _prep_core_inputs(core, inputs, consts)
        in_maps.append({k: d[k] for k in sorted(e.needed)})
    res = run_bass_kernel_spmd(e.nc, in_maps, core_ids=list(range(8)))
    return np.stack([r["out"] for r in res.results], axis=0)


if __name__ == "__main__":
    import reference as R
    inp = R.setup_inputs(0)
    inp = {k: np.asarray(v) for k, v in inp.items()}
    out = kernel(**inp)
    ref = np.asarray(R.reference(**inp))
    err = np.square(out - ref).mean() / np.square(ref).mean()
    print("resid-var ratio:", err)
